# version 23
# baseline (speedup 1.0000x reference)
import numpy as np
import concourse.bass as bass
import concourse.mybir as mybir
from concourse.bass_utils import run_bass_kernel_spmd

F32 = mybir.dt.float32
BF16 = mybir.dt.bfloat16
FP8 = mybir.dt.float8e4
I8 = mybir.dt.int8
AF = mybir.ActivationFunctionType
ALU = mybir.AluOpType
AX = mybir.AxisListType

NCORES = 8
B = 4
T = 16
D = 1024
DC = 8
INNER = 4096
NQ = 4
ROWS = B * T * 128

ENGINES = ("sync", "scalar", "vector", "tensor", "gpsimd")


class Prog:

    def __init__(self):
        self.ops = []
        self.res = {}

    def add(self, eng, fn, reads=(), writes=(), dma=False, ch=None):
        assert (ch is not None) == bool(dma)
        idx = len(self.ops)
        deps = set()
        for k in reads:
            r = self.res.get(k)
            if r and r["w"] is not None:
                deps.add(r["w"])
        for k in writes:
            r = self.res.get(k)
            if r:
                if r["w"] is not None:
                    deps.add(r["w"])
                deps.update(r["r"])
        self.ops.append({"eng": eng, "fn": fn, "dma": dma, "ch": ch, "deps": deps})
        for k in reads:
            self.res.setdefault(k, {"w": None, "r": []})["r"].append(idx)
        for k in writes:
            self.res[k] = {"w": idx, "r": []}
        return idx

    def emit(self, nc, block):
        ops = self.ops
        n = len(ops)
        needs_sem = [set() for _ in range(n)]
        for i, op in enumerate(ops):
            for d in op["deps"]:
                p = ops[d]
                if (p["eng"] == op["eng"] == "tensor"
                        and not p["dma"] and not op["dma"]):
                    continue
                needs_sem[i].add(d)
        milestone = [False] * n
        for i in range(n):
            for d in needs_sem[i]:
                milestone[d] = True
        eng_count = {e: 0 for e in ENGINES}
        ch_count = {}
        for i, op in enumerate(ops):
            if op["dma"]:
                c = op["ch"]
                ch_count[c] = ch_count.get(c, 0) + 1
                op["sem"] = "ch_" + c
                op["val"] = 16 * ch_count[c]
                op["inc"] = 16
            elif milestone[i]:
                eng_count[op["eng"]] += 1
                op["sem"] = "eng_" + op["eng"]
                op["val"] = eng_count[op["eng"]]
                op["inc"] = 1
            else:
                op["sem"] = None
        eng_vc = {e: {} for e in ENGINES}
        waits_out = [[] for _ in range(n)]
        for i, op in enumerate(ops):
            vc = eng_vc[op["eng"]]
            req = {}
            for d in needs_sem[i]:
                pp = ops[d]
                s, v = pp["sem"], pp["val"]
                if vc.get(s, 0) >= v:
                    pass
                elif req.get(s, 0) < v:
                    req[s] = v
            for s, v in sorted(req.items()):
                waits_out[i].append((s, v))
            for d in needs_sem[i]:
                for s2, v2 in ops[d].get("vc_after", {}).items():
                    if vc.get(s2, 0) < v2:
                        vc[s2] = v2
            for s, v in req.items():
                if vc.get(s, 0) < v:
                    vc[s] = v
            va = dict(vc)
            if op["sem"] is not None:
                va[op["sem"]] = op["val"]
            op["vc_after"] = va

        sem_handles = {}
        from contextlib import ExitStack
        self._stack = ExitStack()
        for e in ENGINES:
            sem_handles["eng_" + e] = self._stack.enter_context(nc.semaphore("s_" + e))
        for c in ch_count:
            sem_handles["ch_" + c] = self._stack.enter_context(nc.semaphore("c_" + c))

        def run_engine(name):
            def f(e):
                for i, op in enumerate(ops):
                    if op["eng"] != name:
                        continue
                    for s, v in waits_out[i]:
                        e.wait_ge(sem_handles[s], v)
                    ins = op["fn"](e)
                    if op["sem"] is not None:
                        ins.then_inc(sem_handles[op["sem"]], op["inc"])
            return f

        for e in ENGINES:
            getattr(block, e)(run_engine(e))
        self._stack.close()


def build_nc(debug=False):
    nc = bass.Bass("TRN2", target_bir_lowering=False)
    x_h = nc.dram_tensor("x", [ROWS, D], F32, kind="ExternalInput").ap()
    q_h = nc.dram_tensor("q", [D, D], F32, kind="ExternalInput").ap()
    wg_h = nc.dram_tensor("wg", [INNER, D], F32, kind="ExternalInput").ap()
    wu_h = nc.dram_tensor("wu", [INNER, D], F32, kind="ExternalInput").ap()
    wd_h = nc.dram_tensor("wd", [D, INNER], F32, kind="ExternalInput").ap()
    o_h = nc.dram_tensor("o", [ROWS, D], F32, kind="ExternalOutput").ap()

    scr_xn = nc.dram_tensor("scr_xn", [B * T * 128, D], BF16, kind="Internal").ap()
    scr_xnT = nc.dram_tensor("scr_xnT", [B * T * 128, D], BF16, kind="Internal").ap()
    scr_xqT = nc.dram_tensor("scr_xqT", [B * T * 128, D], BF16, kind="Internal").ap()
    scr_K = nc.dram_tensor("scr_K", [B * DC * 128, D], BF16, kind="Internal").ap()
    if debug:
        dbg_xn = nc.dram_tensor("dbg_xn", [T * 128, D], F32, kind="ExternalOutput").ap()
        dbg_xnT = nc.dram_tensor("dbg_xnT", [T * 128, D], F32, kind="ExternalOutput").ap()
        dbg_xqT = nc.dram_tensor("dbg_xqT", [T * 128, D], F32, kind="ExternalOutput").ap()
        dbg_K = nc.dram_tensor("dbg_K", [DC * 128, D], F32, kind="ExternalOutput").ap()
        dbg_M = nc.dram_tensor("dbg_M", [128, DC * D], F32, kind="ExternalOutput").ap()
        dbg_msw = nc.dram_tensor("dbg_msw", [128, 6], F32, kind="ExternalOutput").ap()
        dbg_wgT = nc.dram_tensor("dbg_wgT", [128, DC * INNER], F32, kind="ExternalOutput").ap()

    p = Prog()
    from contextlib import ExitStack
    with ExitStack() as st:
        e_ = st.enter_context
        wgT = e_(nc.sbuf_tensor("wgT", [128, DC, INNER], FP8))
        wuT = e_(nc.sbuf_tensor("wuT", [128, DC, INNER], FP8))
        wdT = e_(nc.sbuf_tensor("wdT", [128, INNER // 128, D], FP8))
        ident = e_(nc.sbuf_tensor("ident", [128, 128], BF16))
        ones_row = e_(nc.sbuf_tensor("ones_row", [1, 128], F32))
        ones_col = e_(nc.sbuf_tensor("ones_col", [128, 1], F32))
        eps_ap = e_(nc.sbuf_tensor("eps_ap", [128, 1], F32))
        nhalf_ap = e_(nc.sbuf_tensor("nhalf_ap", [128, 1], F32))
        msw_sb = e_(nc.sbuf_tensor("msw_sb", [128, 9], F32))
        x_in = [e_(nc.sbuf_tensor(f"x_in{i}", [128, D], F32)) for i in range(2)]
        xa = [e_(nc.sbuf_tensor(f"xa{i}", [128, D], F32)) for i in range(2)]
        out_b = [e_(nc.sbuf_tensor(f"out_b{i}", [128, D], F32)) for i in range(2)]

        xnf = e_(nc.sbuf_tensor("xnf", [128, D], F32))
        xn_bf = [e_(nc.sbuf_tensor(f"xn_bf{i}", [128, D], BF16)) for i in range(2)]
        i8b = e_(nc.sbuf_tensor("i8b", [128, D], I8))
        r0f = e_(nc.sbuf_tensor("r0f", [128, D], BF16))
        ddb = e_(nc.sbuf_tensor("ddb", [128, D], BF16))
        s1b = e_(nc.sbuf_tensor("s1b", [128, D], BF16))
        xqn2 = [e_(nc.sbuf_tensor(f"xqn{i}", [128, D], BF16)) for i in range(2)]
        xq4 = [e_(nc.sbuf_tensor(f"xq4_{i}", [128, D], BF16)) for i in range(4)]
        tT_buf = [e_(nc.sbuf_tensor(f"tT_buf{i}", [128, D], BF16)) for i in range(2)]
        ast = [e_(nc.sbuf_tensor(f"ast{i}", [128, D], BF16)) for i in range(2)]
        xqT_st = [e_(nc.sbuf_tensor(f"xqT_st{i}", [128, D], BF16)) for i in range(2)]
        h_sb = e_(nc.sbuf_tensor("h_sb", [128, NQ * 1024], BF16))
        sg_sb = e_(nc.sbuf_tensor("sg_sb", [128, 1024], BF16))
        hqT = [e_(nc.sbuf_tensor(f"hqT{i}", [128, 1024], BF16)) for i in range(2)]
        g_sb = e_(nc.sbuf_tensor("g_sb", [128, 1024], BF16))
        M_sb = e_(nc.sbuf_tensor("M_sb", [128, DC, D], BF16))
        bn12 = e_(nc.sbuf_tensor("bn12", [128, 12], F32))
        bnag = e_(nc.sbuf_tensor("bnag", [128, 2], F32))
        sq1 = e_(nc.sbuf_tensor("sq1", [128, 1], F32))
        rstd_all = e_(nc.sbuf_tensor("rstd_all", [128, T], F32))
        mean_all = e_(nc.sbuf_tensor("mean_all", [128, T], F32))
        var_all = e_(nc.sbuf_tensor("var_all", [128, T], F32))
        sqt_all = e_(nc.sbuf_tensor("sqt_all", [128, T], F32))
        nmr_all = e_(nc.sbuf_tensor("nmr_all", [128, T], F32))
        rc1 = e_(nc.sbuf_tensor("rc1", [128, 1], F32))
        iv1 = e_(nc.sbuf_tensor("iv1", [128, 1], F32))
        sqq_all = e_(nc.sbuf_tensor("sqq_all", [128, T], F32))
        Eg_all = e_(nc.sbuf_tensor("Eg_all", [128, T], F32))
        Eu_all = e_(nc.sbuf_tensor("Eu_all", [128, T], F32))
        hm4 = e_(nc.sbuf_tensor("hm4", [128, NQ], F32))
        hm1 = e_(nc.sbuf_tensor("hm1", [128, 1], F32))
        s2c = e_(nc.sbuf_tensor("s2c", [128, 2], F32))
        Edc = e_(nc.sbuf_tensor("Edc", [128, 2], F32))
        wacc = e_(nc.sbuf_tensor("wacc", [128, 2], F32))
        prd = e_(nc.sbuf_tensor("prd", [128, 1], F32))
        sc1 = e_(nc.sbuf_tensor("sc1", [1, 4], F32))
        pd = e_(nc.psum_tensor("pd", [128, 1024], F32))
        pg = e_(nc.psum_tensor("pg", [128, 1024], F32))
        pu = e_(nc.psum_tensor("pu", [128, 1024], F32))
        ptr = e_(nc.psum_tensor("ptr", [128, 1024], BF16))
        spare = e_(nc.psum_tensor("spare", [128, 512], F32))
        block = e_(nc.Block())

        wbufs = [(x_in[0], "x_in0"), (x_in[1], "x_in1"), (out_b[0], "out_b0"), (out_b[1], "out_b1")]

        p.add("vector", lambda e: e.memset(ident[:], 0.0), writes=["ident"])
        p.add("gpsimd", lambda e: e.affine_select(
            out=ident[:], in_=ident[:], compare_op=ALU.not_equal, fill=1.0,
            base=0, pattern=[[-1, 128]], channel_multiplier=1),
            reads=["ident"], writes=["ident"])
        p.add("vector", lambda e: e.memset(ones_row[:], 1.0), writes=["ones_row"])
        p.add("vector", lambda e: e.memset(ones_col[:], 1.0), writes=["ones_col"])
        p.add("vector", lambda e: e.memset(eps_ap[:], 1e-5), writes=["eps_ap"])
        p.add("vector", lambda e: e.memset(nhalf_ap[:], -0.5), writes=["nhalf"])
        _stats0_done = []

        def w_pieces(w_h, tr):
            out = []
            if not tr:
                for ic in range(INNER // 128):
                    out.append(w_h[ic * 128:(ic + 1) * 128, :])
            else:
                for oc in range(D // 128):
                    for qc in range(NQ):
                        out.append(w_h[oc * 128:(oc + 1) * 128, qc * 1024:(qc + 1) * 1024])
            return out

        weights = [("wg", wg_h, False, wgT, "wgT"), ("wu", wu_h, False, wuT, "wuT"),
                   ("wd", wd_h, True, wdT, "wdT")]

        for j, (nm, w_h, tr, wT, wTk) in enumerate(weights):
            pieces = w_pieces(w_h, tr)
            p.add("vector", lambda e: e.memset(wacc[:], 0.0), writes=["wacc"])
            for i, pc in enumerate(pieces):
                q4 = i % 4
                hsl = wbufs[q4][0][:]
                wk = wbufs[q4][1]
                p.add("sync", lambda e, pc=pc, hsl=hsl: e.dma_start(out=hsl, in_=pc),
                      writes=[wk], dma=True, ch=f"wp{q4}")
                p.add("vector", lambda e, hsl=hsl: e.tensor_reduce(
                    prd[:], hsl, AX.X, ALU.add, apply_absolute_value=True),
                    reads=[wk], writes=["prd"])
                a, bcol = i % 2, (i + 1) % 2
                p.add("vector", lambda e, a=a, bcol=bcol: e.tensor_tensor(
                    wacc[:, bcol:bcol + 1], wacc[:, a:a + 1], prd[:], ALU.add),
                    reads=["wacc", "prd"], writes=["wacc"])
            fin = len(pieces) % 2
            p.add("tensor", lambda e, fin=fin: e.matmul(
                spare[0:1, 0:1], wacc[:, fin:fin + 1], ones_col[:], start=True, stop=True),
                reads=["wacc", "ones_col"], writes=["spare"])
            nelem = float(INNER * D)
            p.add("vector", lambda e, nelem=nelem: e.tensor_scalar(
                sc1[0:1, 0:1], spare[0:1, 0:1], 1.0 / nelem, None, ALU.mult),
                reads=["spare"], writes=["sc1"])
            p.add("vector", lambda e: e.tensor_scalar(
                sc1[0:1, 1:2], sc1[0:1, 0:1], 1e-5, None, ALU.max),
                reads=["sc1"], writes=["sc1b"])
            p.add("vector", lambda e: e.reciprocal(sc1[0:1, 2:3], sc1[0:1, 1:2]),
                  reads=["sc1b"], writes=["sc1c"])
            p.add("vector", lambda e: e.tensor_scalar(
                sc1[0:1, 3:4], sc1[0:1, 1:2], 1.0 / 254.0, None, ALU.mult),
                reads=["sc1b"], writes=["sc1d"])
            p.add("vector", lambda e: e.tensor_scalar(
                sc1[0:1, 0:1], sc1[0:1, 2:3], -1.0, None, ALU.mult),
                reads=["sc1c"], writes=["sc1"])
            p.add("tensor", lambda e: e.matmul(
                spare[:, 0:2], ones_row[:], sc1[0:1, 2:4], start=True, stop=True),
                reads=["sc1c", "sc1d", "ones_row"], writes=["spare"])
            p.add("tensor", lambda e: e.matmul(
                spare[:, 2:3], ones_row[:], sc1[0:1, 0:1], start=True, stop=True),
                reads=["sc1", "ones_row"], writes=["spare2"])
            p.add("vector", lambda e, jj=j: e.tensor_copy(msw_sb[:, 3 * jj:3 * jj + 2], spare[:, 0:2]),
                  reads=["spare"], writes=[f"msw{j}"])
            p.add("vector", lambda e, jj=j: e.tensor_copy(msw_sb[:, 3 * jj + 2:3 * jj + 3], spare[:, 2:3]),
                  reads=["spare2"], writes=[f"mswn{j}"])

        def emit_w_piece(j, i):
            nm, w_h, tr, wT, wTk = weights[j]
            pc = w_pieces(w_h, tr)[i]
            inv_ap = msw_sb[:, 3 * j:3 * j + 1]
            ninv_ap = msw_sb[:, 3 * j + 2:3 * j + 3]
            q4 = i % 4
            hsl = wbufs[q4][0][:]
            wk = wbufs[q4][1]
            p.add("sync", lambda e, pc=pc, hsl=hsl: e.dma_start(out=hsl, in_=pc),
                  writes=[wk], dma=True, ch=f"wp{q4}")
            p.add("scalar", lambda e, hsl=hsl, ninv_ap=ninv_ap: e.activation(
                s1b[:], hsl, AF.Sign, bias=nhalf_ap[:], scale=ninv_ap),
                reads=[wk, f"mswn{j}", "nhalf"], writes=["s1b"])
            p.add("scalar", lambda e, hsl=hsl, inv_ap=inv_ap: e.activation(
                ddb[:], hsl, AF.Sign, bias=nhalf_ap[:], scale=inv_ap),
                reads=[wk, f"msw{j}", "nhalf"], writes=["ddb"])
            p.add("vector", lambda e: e.tensor_tensor(
                xq4[3][:], s1b[:], ddb[:], ALU.subtract),
                reads=["ddb", "s1b"], writes=["xq4_3"])
            for blk in range(8):
                p.add("tensor", lambda e, blk=blk: e.transpose(
                    ptr[:, blk * 128:(blk + 1) * 128], xq4[3][:, blk * 128:(blk + 1) * 128], ident[:]),
                    reads=["xq4_3", "ident"], writes=["ptr"])
            if not tr:
                dst = wT[:, 0:DC, i * 128:(i + 1) * 128]
            else:
                oc, qc = divmod(i, NQ)
                dst = wT[:, qc * 8:(qc + 1) * 8, oc * 128:(oc + 1) * 128]
            p.add("vector", lambda e, dst=dst: e.tensor_copy(
                dst, ptr[:].rearrange("p (c n) -> p c n", c=8)),
                reads=["ptr"], writes=[wTk])

        w_order = [(j, i) for j in range(3) for i in range(32)]

        def emit_A_front(b, t):
            bi = t % 2
            row0 = (b * T + t) * 128
            p.add("sync", lambda e, row0=row0, bi=bi: e.dma_start(
                out=xa[bi][:], in_=x_h[row0:row0 + 128, :]),
                writes=[f"xa{bi}"], dma=True, ch=f"xa{bi}")
            p.add("scalar", lambda e, bi=bi, t=t: e.activation(
                xnf[:], xa[bi][:], AF.Identity,
                bias=nmr_all[:, t:t + 1], scale=rstd_all[:, t:t + 1]),
                reads=[f"xa{bi}", "rstd_all", "nmr_all"], writes=["xnf"])
            p.add("scalar", lambda e, bi=bi: e.activation(xn_bf[bi][:], xnf[:], AF.Identity),
                  reads=["xnf"], writes=[f"xn_bf{bi}"])
            p.add("gpsimd", lambda e, row0=row0, bi=bi: e.dma_start(
                out=scr_xn[row0:row0 + 128, :], in_=xn_bf[bi][:]),
                reads=[f"xn_bf{bi}"], writes=[f"scr_xn_{b}_{t}"], dma=True, ch=f"xnbf{bi}")
            if debug and b == 0:
                p.add("gpsimd", lambda e, t=t, bi=bi: e.dma_start(
                    out=dbg_xn[t * 128:(t + 1) * 128, :], in_=xn_bf[bi][:]),
                    reads=[f"xn_bf{bi}"], writes=[f"dxn{t}"], dma=True, ch=f"dbgxn{bi}")
            p.add("vector", lambda e: e.tensor_reduce(
                rc1[:], xnf[:], AX.X, ALU.max, apply_absolute_value=True),
                reads=["xnf"], writes=["rc1"])
            p.add("vector", lambda e: e.tensor_scalar(rc1[:], rc1[:], 1e-5, None, ALU.max),
                  reads=["rc1"], writes=["rc1"])
            p.add("vector", lambda e: e.reciprocal(iv1[:], rc1[:]),
                  reads=["rc1"], writes=["iv1"])
            p.add("vector", lambda e, t=t: e.tensor_scalar(
                sqq_all[:, t:t + 1], iv1[:], 127.0, None, ALU.mult),
                reads=["iv1"], writes=[f"sqq{t}"])
            p.add("vector", lambda e, t=t: e.tensor_scalar(
                Eg_all[:, t:t + 1], rc1[:], msw_sb[:, 1:2], None, ALU.mult),
                reads=["rc1", "msw0"], writes=[f"Eg{t}"])
            p.add("vector", lambda e, t=t: e.tensor_scalar(
                Eu_all[:, t:t + 1], rc1[:], msw_sb[:, 4:5], None, ALU.mult),
                reads=["rc1", "msw1"], writes=[f"Eu{t}"])
            sq_ap = sqq_all[:, t:t + 1]
            p.add("scalar", lambda e, sq_ap=sq_ap, t=t: e.activation(
                i8b[:], xnf[:], AF.Identity, scale=sq_ap),
                reads=["xnf", f"sqq{t}"], writes=["i8b"])
            p.add("scalar", lambda e: e.copy(r0f[:], i8b[:]), reads=["i8b"], writes=["r0f"])
            p.add("vector", lambda e, sq_ap=sq_ap, t=t: e.scalar_tensor_tensor(
                ddb[:], xnf[:], sq_ap, r0f[:], ALU.mult, ALU.subtract),
                reads=["xnf", "r0f", f"sqq{t}"], writes=["ddb"])
            p.add("vector", lambda e: e.scalar_tensor_tensor(
                s1b[:], ddb[:], 0.5, r0f[:], ALU.is_gt, ALU.add),
                reads=["ddb", "r0f"], writes=["s1b"])
            p.add("vector", lambda e, bi=bi: e.scalar_tensor_tensor(
                xqn2[bi][:], ddb[:], -0.5, s1b[:], ALU.is_lt, ALU.subtract),
                reads=["ddb", "s1b"], writes=[f"xqn{bi}"])

        def emit_A_back(b, t):
            bi = t % 2
            row0 = (b * T + t) * 128
            for blk in range(8):
                p.add("tensor", lambda e, blk=blk, bi=bi: e.transpose(
                    ptr[:, blk * 128:(blk + 1) * 128], xn_bf[bi][:, blk * 128:(blk + 1) * 128], ident[:]),
                    reads=[f"xn_bf{bi}", "ident"], writes=["ptr"])
            p.add("vector", lambda e: e.tensor_copy(tT_buf[0][:], ptr[:]),
                  reads=["ptr"], writes=["tT0"])
            p.add("gpsimd", lambda e, row0=row0: e.dma_start(
                out=scr_xnT[row0:row0 + 128, :], in_=tT_buf[0][:]),
                reads=["tT0"], writes=[f"scr_xnT_{b}_{t}"], dma=True, ch="tT0")
            if debug and b == 0:
                p.add("gpsimd", lambda e, t=t: e.dma_start(
                    out=dbg_xnT[t * 128:(t + 1) * 128, :], in_=tT_buf[0][:]),
                    reads=["tT0"], writes=[f"dxnT{t}"], dma=True, ch="dbgT0")
            for blk in range(8):
                p.add("tensor", lambda e, blk=blk, bi=bi: e.transpose(
                    ptr[:, blk * 128:(blk + 1) * 128], xqn2[bi][:, blk * 128:(blk + 1) * 128], ident[:]),
                    reads=[f"xqn{bi}", "ident"], writes=["ptr"])
            p.add("vector", lambda e: e.tensor_copy(tT_buf[1][:], ptr[:]),
                  reads=["ptr"], writes=["tT1"])
            p.add("gpsimd", lambda e, row0=row0: e.dma_start(
                out=scr_xqT[row0:row0 + 128, :], in_=tT_buf[1][:]),
                reads=["tT1"], writes=[f"scr_xqT_{b}_{t}"], dma=True, ch="tT1")
            if debug and b == 0:
                p.add("gpsimd", lambda e, t=t: e.dma_start(
                    out=dbg_xqT[t * 128:(t + 1) * 128, :], in_=tT_buf[1][:]),
                    reads=["tT1"], writes=[f"dxqT{t}"], dma=True, ch="dbgT1")

        def emit_A_tile(b, t):
            emit_A_front(b, t)
            emit_A_back(b, t)

        def emit_A_stats(b):
            for t in range(T):
                bi = t % 2
                row0 = (b * T + t) * 128
                p.add("sync", lambda e, row0=row0, bi=bi: e.dma_start(
                    out=xa[bi][:], in_=x_h[row0:row0 + 128, :]),
                    writes=[f"xa{bi}"], dma=True, ch=f"xa{bi}")
                p.add("vector", lambda e, bi=bi: e.bn_stats(bn12[:, 0:6], xa[bi][:, 0:512]),
                      reads=[f"xa{bi}"], writes=["bn12a"])
                p.add("vector", lambda e, bi=bi: e.bn_stats(bn12[:, 6:12], xa[bi][:, 512:1024]),
                      reads=[f"xa{bi}"], writes=["bn12b"])
                p.add("vector", lambda e: e.bn_aggr(bnag[:], bn12[:]),
                      reads=["bn12a", "bn12b"], writes=["bnag"])
                p.add("vector", lambda e, t=t: e.tensor_copy(mean_all[:, t:t + 1], bnag[:, 0:1]),
                      reads=["bnag"], writes=["mean_all"])
                p.add("vector", lambda e, t=t: e.tensor_copy(var_all[:, t:t + 1], bnag[:, 1:2]),
                      reads=["bnag"], writes=["var_all"])
            p.add("scalar", lambda e: e.activation(sqt_all[:], var_all[:], AF.Sqrt, bias=eps_ap[:]),
                  reads=["var_all", "eps_ap"], writes=["sqt_all"])
            p.add("vector", lambda e: e.reciprocal(rstd_all[:], sqt_all[:]),
                  reads=["sqt_all"], writes=["rstd_all"])
            p.add("vector", lambda e: e.scalar_tensor_tensor(
                nmr_all[:], mean_all[:], -1.0, rstd_all[:], ALU.mult, ALU.mult),
                reads=["mean_all", "rstd_all"], writes=["nmr_all"])

        groups = [(0, 1, 2), (3, 4, 5), (6, 7)]

        kq_bufs = [(ast[0], "ast0", "ast0"), (ast[1], "ast1", "ast1"),
                   (xqT_st[0], "xqT_st0", "xq0"), (xqT_st[1], "xqT_st1", "xq1")]

        def emit_K(b):
            for gi, grp in enumerate(groups):
                for t in range(T):
                    bi = t % 2
                    row0 = (b * T + t) * 128
                    p.add("sync", lambda e, row0=row0, bi=bi: e.dma_start(
                        out=ast[bi][:], in_=scr_xn[row0:row0 + 128, :]),
                        reads=[f"scr_xn_{b}_{t}"], writes=[f"ast{bi}"], dma=True, ch=f"ast{bi}")
                    for ji, c in enumerate(grp):
                        ps = (pd, pg, pu)[ji]
                        pk = ("pd", "pg", "pu")[ji]
                        for h2 in range(2):
                            p.add("tensor", lambda e, c=c, h2=h2, bi=bi, ps=ps, t=t: e.matmul(
                                ps[:, h2 * 512:(h2 + 1) * 512],
                                ast[bi][:, c * 128:(c + 1) * 128],
                                ast[bi][:, h2 * 512:(h2 + 1) * 512],
                                start=(t == 0), stop=(t == T - 1)),
                                reads=[f"ast{bi}"], writes=[pk])
                for ji, c in enumerate(grp):
                    ps = (pd, pg, pu)[ji]
                    pk = ("pd", "pg", "pu")[ji]
                    bi = ji % 2
                    p.add("scalar", lambda e, ps=ps, bi=bi: e.copy(hqT[bi][:], ps[:]),
                          reads=[pk], writes=[f"hqT{bi}"])
                    krow = (b * DC + c) * 128
                    p.add("gpsimd", lambda e, krow=krow, bi=bi: e.dma_start(
                        out=scr_K[krow:krow + 128, :], in_=hqT[bi][:]),
                        reads=[f"hqT{bi}"], writes=[f"scr_K_{b}_{c}"], dma=True, ch=f"kev{bi}")
                    if debug and b == 0:
                        p.add("gpsimd", lambda e, c=c, bi=bi: e.dma_start(
                            out=dbg_K[c * 128:(c + 1) * 128, :], in_=hqT[bi][:]),
                            reads=[f"hqT{bi}"], writes=[f"dK{c}"], dma=True, ch=f"dbgK{bi}")

        def emit_M(b):
            for gi, grp in enumerate(groups):
                c0 = grp[0]
                w = len(grp)
                for k in range(DC):
                    bi = k % 2
                    buf, bk, bch = (ast[0], "ast0", "ast0") if k % 2 == 0 else (ast[1], "ast1", "ast1")
                    p.add("sync", lambda e, k=k, c0=c0, w=w, bi=bi: e.dma_start(
                        out=xa[bi][:, 0:w * 128], in_=q_h[k * 128:(k + 1) * 128, c0 * 128:(c0 + w) * 128]),
                        writes=[f"xa{bi}"], dma=True, ch=f"xa{bi}")
                    p.add("scalar", lambda e, bi=bi, w=w: e.activation(
                        tT_buf[bi][:, 0:w * 128], xa[bi][:, 0:w * 128], AF.Identity),
                        reads=[f"xa{bi}"], writes=[f"tT{bi}"])
                    krow = (b * DC + k) * 128
                    p.add("sync", lambda e, krow=krow, buf=buf: e.dma_start(
                        out=buf[:], in_=scr_K[krow:krow + 128, :]),
                        reads=[f"scr_K_{b}_{k}"], writes=[bk], dma=True, ch=bch)
                    for ji, c in enumerate(grp):
                        ps = (pd, pg, pu)[ji]
                        pk = ("pd", "pg", "pu")[ji]
                        for h2 in range(2):
                            p.add("tensor", lambda e, ji=ji, h2=h2, bi=bi, ps=ps, buf=buf, k=k: e.matmul(
                                ps[:, h2 * 512:(h2 + 1) * 512],
                                tT_buf[bi][:, ji * 128:(ji + 1) * 128],
                                buf[:, h2 * 512:(h2 + 1) * 512],
                                start=(k == 0), stop=(k == DC - 1)),
                                reads=[f"tT{bi}", bk], writes=[pk])
                for ji, c in enumerate(grp):
                    ps = (pd, pg, pu)[ji]
                    pk = ("pd", "pg", "pu")[ji]
                    p.add("scalar", lambda e, ps=ps, c=c: e.copy(M_sb[:, c, :], ps[:]),
                          reads=[pk], writes=[f"M{c}"])
            if debug and b == 0:
                p.add("gpsimd", lambda e: e.dma_start(
                    out=dbg_M, in_=M_sb[:].rearrange("p c d -> p (c d)")),
                    reads=[f"M{c}" for c in range(DC)], writes=["dbg3"], dma=True, ch="dbgM")

        def fs_dmas(b, t):
            bi = t % 2
            row0 = (b * T + t) * 128
            p.add("sync", lambda e, row0=row0, bi=bi: e.dma_start(
                out=xqT_st[bi][:], in_=scr_xqT[row0:row0 + 128, :]),
                reads=[f"scr_xqT_{b}_{t}"], writes=[f"xqT_st{bi}"], dma=True, ch=f"xq{bi}")
            p.add("sync", lambda e, row0=row0, bi=bi: e.dma_start(
                out=ast[bi][:], in_=scr_xnT[row0:row0 + 128, :]),
                reads=[f"scr_xnT_{b}_{t}"], writes=[f"ast{bi}"], dma=True, ch=f"ast{bi}")
            p.add("sync", lambda e, row0=row0, bi=bi: e.dma_start(
                out=x_in[bi][:], in_=x_h[row0:row0 + 128, :]),
                writes=[f"x_in{bi}"], dma=True, ch=f"xin{bi}")

        def fs_B_chain(t, qu):
            p2 = t % 2
            s2_ap = s2c[:, p2:p2 + 1]
            hq_sl = h_sb[:, qu * 1024:(qu + 1) * 1024]
            p.add("scalar", lambda e, hq_sl=hq_sl, s2_ap=s2_ap: e.activation(
                i8b[:], hq_sl, AF.Identity, scale=s2_ap),
                reads=[f"h{qu}", f"s2c{p2}"], writes=["i8b"])
            p.add("scalar", lambda e: e.copy(r0f[:], i8b[:]), reads=["i8b"], writes=["r0f"])
            p.add("vector", lambda e, hq_sl=hq_sl, s2_ap=s2_ap: e.scalar_tensor_tensor(
                ddb[:], hq_sl, s2_ap, r0f[:], ALU.mult, ALU.subtract),
                reads=[f"h{qu}", "r0f", f"s2c{p2}"], writes=["ddb"])
            p.add("vector", lambda e: e.scalar_tensor_tensor(
                s1b[:], ddb[:], 0.5, r0f[:], ALU.is_gt, ALU.add),
                reads=["ddb", "r0f"], writes=["s1b"])
            p.add("vector", lambda e, qu=qu: e.scalar_tensor_tensor(
                xq4[qu][:], ddb[:], -0.5, s1b[:], ALU.is_lt, ALU.subtract),
                reads=["ddb", "s1b"], writes=[f"xq4_{qu}"])

        def fs_gateup(b, t, qu):
            bi = t % 2
            for h2 in range(2):
                for dc in range(DC):
                    p.add("tensor", lambda e, qu=qu, h2=h2, dc=dc, bi=bi: e.matmul(
                        pg[:, h2 * 512:(h2 + 1) * 512],
                        xqT_st[bi][:, dc * 128:(dc + 1) * 128],
                        wgT[:, dc, qu * 1024 + h2 * 512: qu * 1024 + (h2 + 1) * 512],
                        start=(dc == 0), stop=(dc == DC - 1)),
                        reads=[f"xqT_st{bi}", "wgT"], writes=["pg"])
            for h2 in range(2):
                for dc in range(DC):
                    p.add("tensor", lambda e, qu=qu, h2=h2, dc=dc, bi=bi: e.matmul(
                        pu[:, h2 * 512:(h2 + 1) * 512],
                        xqT_st[bi][:, dc * 128:(dc + 1) * 128],
                        wuT[:, dc, qu * 1024 + h2 * 512: qu * 1024 + (h2 + 1) * 512],
                        start=(dc == 0), stop=(dc == DC - 1)),
                        reads=[f"xqT_st{bi}", "wuT"], writes=["pu"])

        def fs_tr(t, qu):
            for blk in range(8):
                p.add("tensor", lambda e, qu=qu, blk=blk: e.transpose(
                    ptr[:, blk * 128:(blk + 1) * 128], xq4[qu][:, blk * 128:(blk + 1) * 128], ident[:]),
                    reads=[f"xq4_{qu}", "ident"], writes=["ptr"])
            hb = qu % 2
            p.add("vector", lambda e, hb=hb: e.tensor_copy(hqT[hb][:], ptr[:]),
                  reads=["ptr"], writes=[f"hqT{hb}"])

        def fs_down(t, qu):
            hb = qu % 2
            for h2 in range(2):
                for jj in range(8):
                    p.add("tensor", lambda e, qu=qu, h2=h2, jj=jj, hb=hb: e.matmul(
                        pd[:, h2 * 512:(h2 + 1) * 512],
                        hqT[hb][:, jj * 128:(jj + 1) * 128],
                        wdT[:, qu * 8 + jj, h2 * 512:(h2 + 1) * 512],
                        start=(qu == 0 and jj == 0), stop=(qu == NQ - 1 and jj == 7)),
                        reads=[f"hqT{hb}", "wdT"], writes=["pd"])

        def fs_sgh(t, qu):
            p.add("scalar", lambda e, t=t: e.activation(
                sg_sb[:], pg[:], AF.Silu, scale=Eg_all[:, t:t + 1]),
                reads=["pg", f"Eg{t}"], writes=["sg"])
            p.add("vector", lambda e, qu=qu, t=t: e.scalar_tensor_tensor(
                h_sb[:, qu * 1024:(qu + 1) * 1024], pu[:], Eu_all[:, t:t + 1], sg_sb[:],
                ALU.mult, ALU.mult),
                reads=["pu", "sg", f"Eu{t}"], writes=[f"h{qu}"])
            p.add("vector", lambda e, qu=qu: e.tensor_reduce(
                hm4[:, qu:qu + 1], h_sb[:, qu * 1024:(qu + 1) * 1024], AX.X,
                ALU.max, apply_absolute_value=True),
                reads=[f"h{qu}"], writes=["hm4"])

        def fs_F_tail(t):
            p2 = t % 2
            p.add("vector", lambda e: e.tensor_reduce(hm1[:], hm4[:], AX.X, ALU.max),
                  reads=["hm4"], writes=["hm1"])
            p.add("vector", lambda e: e.tensor_scalar(hm1[:], hm1[:], 1e-5, None, ALU.max),
                  reads=["hm1"], writes=["hm1"])
            p.add("vector", lambda e, p2=p2: e.reciprocal(s2c[:, p2:p2 + 1], hm1[:]),
                  reads=["hm1"], writes=[f"s2c{p2}"])
            p.add("vector", lambda e, p2=p2: e.tensor_scalar(
                s2c[:, p2:p2 + 1], s2c[:, p2:p2 + 1], 127.0, None, ALU.mult),
                reads=[f"s2c{p2}"], writes=[f"s2c{p2}"])
            p.add("vector", lambda e, p2=p2: e.tensor_scalar(
                Edc[:, p2:p2 + 1], hm1[:], msw_sb[:, 7:8], None, ALU.mult),
                reads=["hm1", "msw2"], writes=[f"Edc{p2}"])

        def fs_B_g(b, t):
            p2 = t % 2
            p.add("scalar", lambda e, p2=p2: e.activation(
                g_sb[:], pd[:], AF.Silu, scale=Edc[:, p2:p2 + 1]),
                reads=["pd", f"Edc{p2}"], writes=["g_sb"])

        def fs_B_tail(b, t):
            bi = t % 2
            p2 = t % 2
            row0 = (b * T + t) * 128
            for h2 in range(2):
                for dc in range(DC):
                    p.add("tensor", lambda e, h2=h2, dc=dc, bi=bi: e.matmul(
                        spare[:],
                        ast[bi][:, dc * 128:(dc + 1) * 128],
                        M_sb[:, dc, h2 * 512:(h2 + 1) * 512],
                        start=(dc == 0), stop=(dc == DC - 1)),
                        reads=[f"ast{bi}", f"M{dc}"], writes=["spare"])
                p.add("vector", lambda e, h2=h2: e.scalar_tensor_tensor(
                    ddb[:, h2 * 512:(h2 + 1) * 512], spare[:], 1.0,
                    g_sb[:, h2 * 512:(h2 + 1) * 512], ALU.mult, ALU.mult),
                    reads=["spare", "g_sb"], writes=["ddb"])
            p.add("vector", lambda e, bi=bi: e.tensor_tensor(
                out_b[bi][:], ddb[:], x_in[bi][:], ALU.add),
                reads=["ddb", f"x_in{bi}"], writes=[f"out_b{bi}"])
            p.add("gpsimd", lambda e, row0=row0, bi=bi: e.dma_start(
                out=o_h[row0:row0 + 128, :], in_=out_b[bi][:]),
                reads=[f"out_b{bi}"], writes=[f"hbm_out_{b}_{t}"], dma=True, ch=f"out{bi}")

        def emit_FS(b, next_a):
            fs_dmas(b, 0)
            for slot in range(T + 1):
                tf, tb = slot, slot - 1
                for step in range(NQ + 1):
                    if step < NQ and tf < T:
                        fs_gateup(b, tf, step)
                    if tb >= 0 and step >= 1:
                        fs_down(tb, step - 1)
                    if tb >= 0 and step < NQ:
                        fs_tr(tb, step)
                    if step < NQ and tf < T:
                        fs_sgh(tf, step)
                if tf < T:
                    fs_F_tail(tf)
                if tb >= 0:
                    fs_B_g(b, tb)
                if tf < T:
                    for qu in range(NQ):
                        fs_B_chain(tf, qu)
                if next_a and slot >= 1:
                    emit_A_back(b + 1, slot - 1)
                if tb >= 0:
                    fs_B_tail(b, tb)
                if next_a and slot < T:
                    emit_A_front(b + 1, slot)
                if tf + 1 < T:
                    fs_dmas(b, tf + 1)

        emit_A_stats(0)
        wi = 0
        for t in range(T):
            emit_A_tile(0, t)
            for _ in range(6):
                if wi < len(w_order):
                    emit_w_piece(*w_order[wi])
                    wi += 1
        while wi < len(w_order):
            emit_w_piece(*w_order[wi])
            wi += 1
        if debug:
            p.add("sync", lambda e: e.dma_start(out=dbg_msw, in_=msw_sb[:]),
                  reads=["msw0", "msw1", "msw2"], writes=["dbg2"], dma=True, ch="dbgB")
            p.add("gpsimd", lambda e: e.dma_start(
                out=dbg_wgT, in_=wgT[:].rearrange("p c d -> p (c d)")),
                reads=["wgT"], writes=["dbg1"], dma=True, ch="dbgA")
        emit_K(0)
        emit_M(0)
        for b in range(B):
            if b + 1 < B:
                emit_A_stats(b + 1)
            emit_FS(b, next_a=(b + 1 < B))
            if b + 1 < B:
                emit_K(b + 1)
                emit_M(b + 1)

        p.add("sync", lambda e: e.nop(),
              reads=[f"hbm_out_{b}_{t}" for b in range(B) for t in range(T)])

        p.emit(nc, block)
    return nc


_NC_CACHE = {}


def kernel(L0_attn_hidden_states, L0_attn_queries, L0_ffn_w_gate, L0_ffn_w_up, L0_ffn_w_down):
    x = np.ascontiguousarray(np.asarray(L0_attn_hidden_states, dtype=np.float32))
    q = np.ascontiguousarray(np.asarray(L0_attn_queries, dtype=np.float32))
    wg = np.ascontiguousarray(np.asarray(L0_ffn_w_gate, dtype=np.float32))
    wu = np.ascontiguousarray(np.asarray(L0_ffn_w_up, dtype=np.float32))
    wd = np.ascontiguousarray(np.asarray(L0_ffn_w_down, dtype=np.float32))
    xs = x.reshape(NCORES, B * 2048, D)
    if "nc" not in _NC_CACHE:
        _NC_CACHE["nc"] = build_nc()
    nc = _NC_CACHE["nc"]
    in_maps = [{"x": np.ascontiguousarray(xs[i]), "q": q, "wg": wg, "wu": wu, "wd": wd}
               for i in range(NCORES)]
    res = run_bass_kernel_spmd(nc, in_maps, core_ids=list(range(NCORES)))
    out = np.concatenate([r["o"].reshape(B, 2048, D) for r in res.results], axis=0)
    return out.astype(np.float32)


# version 25
# speedup vs baseline: 1.1678x; 1.1678x over previous
import numpy as np
import concourse.bass as bass
import concourse.mybir as mybir
from concourse.bass_utils import run_bass_kernel_spmd

F32 = mybir.dt.float32
BF16 = mybir.dt.bfloat16
FP8 = mybir.dt.float8e4
I8 = mybir.dt.int8
AF = mybir.ActivationFunctionType
ALU = mybir.AluOpType
AX = mybir.AxisListType

NCORES = 8
B = 4
T = 16
D = 1024
DC = 8
INNER = 4096
NQ = 4
ROWS = B * T * 128

ENGINES = ("sync", "scalar", "vector", "tensor", "gpsimd")


class Prog:

    def __init__(self):
        self.ops = []
        self.res = {}

    def add(self, eng, fn, reads=(), writes=(), dma=False, ch=None):
        assert (ch is not None) == bool(dma)
        idx = len(self.ops)
        deps = set()
        for k in reads:
            r = self.res.get(k)
            if r and r["w"] is not None:
                deps.add(r["w"])
        for k in writes:
            r = self.res.get(k)
            if r:
                if r["w"] is not None:
                    deps.add(r["w"])
                deps.update(r["r"])
        self.ops.append({"eng": eng, "fn": fn, "dma": dma, "ch": ch, "deps": deps})
        for k in reads:
            self.res.setdefault(k, {"w": None, "r": []})["r"].append(idx)
        for k in writes:
            self.res[k] = {"w": idx, "r": []}
        return idx

    def emit(self, nc, block):
        ops = self.ops
        n = len(ops)
        needs_sem = [set() for _ in range(n)]
        for i, op in enumerate(ops):
            for d in op["deps"]:
                p = ops[d]
                if (p["eng"] == op["eng"] == "tensor"
                        and not p["dma"] and not op["dma"]):
                    continue
                needs_sem[i].add(d)
        milestone = [False] * n
        for i in range(n):
            for d in needs_sem[i]:
                milestone[d] = True
        eng_count = {e: 0 for e in ENGINES}
        ch_count = {}
        for i, op in enumerate(ops):
            if op["dma"]:
                c = op["ch"]
                ch_count[c] = ch_count.get(c, 0) + 1
                op["sem"] = "ch_" + c
                op["val"] = 16 * ch_count[c]
                op["inc"] = 16
            elif milestone[i]:
                eng_count[op["eng"]] += 1
                op["sem"] = "eng_" + op["eng"]
                op["val"] = eng_count[op["eng"]]
                op["inc"] = 1
            else:
                op["sem"] = None
        eng_vc = {e: {} for e in ENGINES}
        waits_out = [[] for _ in range(n)]
        for i, op in enumerate(ops):
            vc = eng_vc[op["eng"]]
            req = {}
            for d in needs_sem[i]:
                pp = ops[d]
                s, v = pp["sem"], pp["val"]
                if vc.get(s, 0) >= v:
                    pass
                elif req.get(s, 0) < v:
                    req[s] = v
            for s, v in sorted(req.items()):
                waits_out[i].append((s, v))
            for d in needs_sem[i]:
                for s2, v2 in ops[d].get("vc_after", {}).items():
                    if vc.get(s2, 0) < v2:
                        vc[s2] = v2
            for s, v in req.items():
                if vc.get(s, 0) < v:
                    vc[s] = v
            va = dict(vc)
            if op["sem"] is not None:
                va[op["sem"]] = op["val"]
            op["vc_after"] = va

        sem_handles = {}
        from contextlib import ExitStack
        self._stack = ExitStack()
        for e in ENGINES:
            sem_handles["eng_" + e] = self._stack.enter_context(nc.semaphore("s_" + e))
        for c in ch_count:
            sem_handles["ch_" + c] = self._stack.enter_context(nc.semaphore("c_" + c))

        def run_engine(name):
            def f(e):
                for i, op in enumerate(ops):
                    if op["eng"] != name:
                        continue
                    for s, v in waits_out[i]:
                        e.wait_ge(sem_handles[s], v)
                    ins = op["fn"](e)
                    if op["sem"] is not None:
                        ins.then_inc(sem_handles[op["sem"]], op["inc"])
            return f

        for e in ENGINES:
            getattr(block, e)(run_engine(e))
        self._stack.close()


def build_nc(debug=False):
    nc = bass.Bass("TRN2", target_bir_lowering=False)
    x_h = nc.dram_tensor("x", [ROWS, D], F32, kind="ExternalInput").ap()
    q_h = nc.dram_tensor("q", [D, D], F32, kind="ExternalInput").ap()
    wg_h = nc.dram_tensor("wg", [INNER, D], F32, kind="ExternalInput").ap()
    wu_h = nc.dram_tensor("wu", [INNER, D], F32, kind="ExternalInput").ap()
    wd_h = nc.dram_tensor("wd", [D, INNER], F32, kind="ExternalInput").ap()
    o_h = nc.dram_tensor("o", [ROWS, D], F32, kind="ExternalOutput").ap()

    scr_xn = nc.dram_tensor("scr_xn", [B * T * 128, D], BF16, kind="Internal").ap()
    scr_xnT = nc.dram_tensor("scr_xnT", [B * T * 128, D], BF16, kind="Internal").ap()
    scr_xqT = nc.dram_tensor("scr_xqT", [B * T * 128, D], BF16, kind="Internal").ap()
    scr_K = nc.dram_tensor("scr_K", [B * DC * 128, D], BF16, kind="Internal").ap()
    if debug:
        dbg_xn = nc.dram_tensor("dbg_xn", [T * 128, D], F32, kind="ExternalOutput").ap()
        dbg_xnT = nc.dram_tensor("dbg_xnT", [T * 128, D], F32, kind="ExternalOutput").ap()
        dbg_xqT = nc.dram_tensor("dbg_xqT", [T * 128, D], F32, kind="ExternalOutput").ap()
        dbg_K = nc.dram_tensor("dbg_K", [DC * 128, D], F32, kind="ExternalOutput").ap()
        dbg_M = nc.dram_tensor("dbg_M", [128, DC * D], F32, kind="ExternalOutput").ap()
        dbg_msw = nc.dram_tensor("dbg_msw", [128, 9], F32, kind="ExternalOutput").ap()
        dbg_wgT = nc.dram_tensor("dbg_wgT", [128, DC * INNER], F32, kind="ExternalOutput").ap()

    p = Prog()
    from contextlib import ExitStack
    with ExitStack() as st:
        e_ = st.enter_context
        wgT = e_(nc.sbuf_tensor("wgT", [128, DC, INNER], FP8))
        wuT = e_(nc.sbuf_tensor("wuT", [128, DC, INNER], FP8))
        wdT = e_(nc.sbuf_tensor("wdT", [128, INNER // 128, D], FP8))
        ident = e_(nc.sbuf_tensor("ident", [128, 128], BF16))
        ones_row = e_(nc.sbuf_tensor("ones_row", [1, 128], F32))
        ones_col = e_(nc.sbuf_tensor("ones_col", [128, 1], F32))
        eps_ap = e_(nc.sbuf_tensor("eps_ap", [128, 1], F32))
        nhalf_ap = e_(nc.sbuf_tensor("nhalf_ap", [128, 1], F32))
        msw_sb = e_(nc.sbuf_tensor("msw_sb", [128, 9], F32))
        x_in = [e_(nc.sbuf_tensor(f"x_in{i}", [128, D], F32)) for i in range(2)]
        xa = [e_(nc.sbuf_tensor(f"xa{i}", [128, D], F32)) for i in range(2)]
        out_b = [e_(nc.sbuf_tensor(f"out_b{i}", [128, D], F32)) for i in range(2)]

        xnf = e_(nc.sbuf_tensor("xnf", [128, D], F32))
        xn_bf = [e_(nc.sbuf_tensor(f"xn_bf{i}", [128, D], BF16)) for i in range(2)]
        i8b = e_(nc.sbuf_tensor("i8b", [128, D], I8))
        r0f = e_(nc.sbuf_tensor("r0f", [128, D], BF16))
        ddb = e_(nc.sbuf_tensor("ddb", [128, D], BF16))
        s1b = e_(nc.sbuf_tensor("s1b", [128, D], BF16))
        xqn2 = [e_(nc.sbuf_tensor(f"xqn{i}", [128, D], BF16)) for i in range(2)]
        xq4 = [e_(nc.sbuf_tensor(f"xq4_{i}", [128, D], BF16)) for i in range(4)]
        tT_buf = [e_(nc.sbuf_tensor(f"tT_buf{i}", [128, D], BF16)) for i in range(2)]
        ast = [e_(nc.sbuf_tensor(f"ast{i}", [128, D], BF16)) for i in range(2)]
        xqT_st = [e_(nc.sbuf_tensor(f"xqT_st{i}", [128, D], BF16)) for i in range(2)]
        h_sb = e_(nc.sbuf_tensor("h_sb", [128, NQ * 1024], BF16))
        sg_sb = e_(nc.sbuf_tensor("sg_sb", [128, 1024], BF16))
        hqT = [e_(nc.sbuf_tensor(f"hqT{i}", [128, 1024], BF16)) for i in range(2)]
        g_sb = e_(nc.sbuf_tensor("g_sb", [128, 1024], BF16))
        M_sb = e_(nc.sbuf_tensor("M_sb", [128, DC, D], BF16))
        bn12 = e_(nc.sbuf_tensor("bn12", [128, 12], F32))
        bnag = e_(nc.sbuf_tensor("bnag", [128, 2], F32))
        sq1 = e_(nc.sbuf_tensor("sq1", [128, 1], F32))
        rstd_all = e_(nc.sbuf_tensor("rstd_all", [128, T], F32))
        mean_all = e_(nc.sbuf_tensor("mean_all", [128, T], F32))
        var_all = e_(nc.sbuf_tensor("var_all", [128, T], F32))
        sqt_all = e_(nc.sbuf_tensor("sqt_all", [128, T], F32))
        rowm_all = e_(nc.sbuf_tensor("rowm_all", [128, T], F32))
        nmr_all = e_(nc.sbuf_tensor("nmr_all", [128, T], F32))
        rc1 = e_(nc.sbuf_tensor("rc1", [128, 1], F32))
        iv1 = e_(nc.sbuf_tensor("iv1", [128, 1], F32))
        sqq_all = e_(nc.sbuf_tensor("sqq_all", [128, T], F32))
        Eg_all = e_(nc.sbuf_tensor("Eg_all", [128, T], F32))
        Eu_all = e_(nc.sbuf_tensor("Eu_all", [128, T], F32))
        hm4 = e_(nc.sbuf_tensor("hm4", [128, NQ], F32))
        hm1 = e_(nc.sbuf_tensor("hm1", [128, 1], F32))
        s2c = e_(nc.sbuf_tensor("s2c", [128, 2], F32))
        Edc = e_(nc.sbuf_tensor("Edc", [128, 2], F32))
        wacc = e_(nc.sbuf_tensor("wacc", [128, 2], F32))
        prd = e_(nc.sbuf_tensor("prd", [128, 1], F32))
        sc1 = e_(nc.sbuf_tensor("sc1", [1, 4], F32))
        pd = e_(nc.psum_tensor("pd", [128, 1024], F32))
        pg = e_(nc.psum_tensor("pg", [128, 1024], F32))
        pu = e_(nc.psum_tensor("pu", [128, 1024], F32))
        ptr = e_(nc.psum_tensor("ptr", [128, 1024], BF16))
        spare = e_(nc.psum_tensor("spare", [128, 512], F32))
        block = e_(nc.Block())

        wbufs = [(x_in[0], "x_in0"), (x_in[1], "x_in1"), (out_b[0], "out_b0"), (out_b[1], "out_b1")]

        p.add("vector", lambda e: e.memset(ident[:], 0.0), writes=["ident"])
        p.add("gpsimd", lambda e: e.affine_select(
            out=ident[:], in_=ident[:], compare_op=ALU.not_equal, fill=1.0,
            base=0, pattern=[[-1, 128]], channel_multiplier=1),
            reads=["ident"], writes=["ident"])
        p.add("vector", lambda e: e.memset(ones_row[:], 1.0), writes=["ones_row"])
        p.add("vector", lambda e: e.memset(ones_col[:], 1.0), writes=["ones_col"])
        p.add("vector", lambda e: e.memset(eps_ap[:], 1e-5), writes=["eps_ap"])
        p.add("vector", lambda e: e.memset(nhalf_ap[:], -0.5), writes=["nhalf"])
        _stats0_done = []

        def w_pieces(w_h, tr):
            out = []
            if not tr:
                for ic in range(INNER // 128):
                    out.append(w_h[ic * 128:(ic + 1) * 128, :])
            else:
                for oc in range(D // 128):
                    for qc in range(NQ):
                        out.append(w_h[oc * 128:(oc + 1) * 128, qc * 1024:(qc + 1) * 1024])
            return out

        weights = [("wg", wg_h, False, wgT, "wgT"), ("wu", wu_h, False, wuT, "wuT"),
                   ("wd", wd_h, True, wdT, "wdT")]

        def emit_p1_piece(j, i):
            nm, w_h, tr, wT, wTk = weights[j]
            pc = w_pieces(w_h, tr)[i]
            if i == 0:
                p.add("vector", lambda e: e.memset(wacc[:], 0.0), writes=["wacc"])
            q4 = i % 4
            hsl = wbufs[q4][0][:]
            wk = wbufs[q4][1]
            p.add("sync", lambda e, pc=pc, hsl=hsl: e.dma_start(out=hsl, in_=pc),
                  writes=[wk], dma=True, ch=f"wp{q4}")
            p.add("vector", lambda e, hsl=hsl: e.tensor_reduce(
                prd[:], hsl, AX.X, ALU.add, apply_absolute_value=True),
                reads=[wk], writes=["prd"])
            a, bcol = i % 2, (i + 1) % 2
            p.add("vector", lambda e, a=a, bcol=bcol: e.tensor_tensor(
                wacc[:, bcol:bcol + 1], wacc[:, a:a + 1], prd[:], ALU.add),
                reads=["wacc", "prd"], writes=["wacc"])

        def emit_p1_final(j):
            p.add("tensor", lambda e: e.matmul(
                spare[0:1, 0:1], wacc[:, 0:1], ones_col[:], start=True, stop=True),
                reads=["wacc", "ones_col"], writes=["spare"])
            nelem = float(INNER * D)
            p.add("vector", lambda e, nelem=nelem: e.tensor_scalar(
                sc1[0:1, 0:1], spare[0:1, 0:1], 1.0 / nelem, None, ALU.mult),
                reads=["spare"], writes=["sc1"])
            p.add("vector", lambda e: e.tensor_scalar(
                sc1[0:1, 1:2], sc1[0:1, 0:1], 1e-5, None, ALU.max),
                reads=["sc1"], writes=["sc1b"])
            p.add("vector", lambda e: e.reciprocal(sc1[0:1, 2:3], sc1[0:1, 1:2]),
                  reads=["sc1b"], writes=["sc1c"])
            p.add("vector", lambda e: e.tensor_scalar(
                sc1[0:1, 3:4], sc1[0:1, 1:2], 1.0 / 254.0, None, ALU.mult),
                reads=["sc1b"], writes=["sc1d"])
            p.add("vector", lambda e: e.tensor_scalar(
                sc1[0:1, 0:1], sc1[0:1, 2:3], -1.0, None, ALU.mult),
                reads=["sc1c"], writes=["sc1"])
            p.add("tensor", lambda e: e.matmul(
                spare[:, 0:2], ones_row[:], sc1[0:1, 2:4], start=True, stop=True),
                reads=["sc1c", "sc1d", "ones_row"], writes=["spare"])
            p.add("tensor", lambda e: e.matmul(
                spare[:, 2:3], ones_row[:], sc1[0:1, 0:1], start=True, stop=True),
                reads=["sc1", "ones_row"], writes=["spare2"])
            p.add("vector", lambda e, jj=j: e.tensor_copy(msw_sb[:, 3 * jj:3 * jj + 2], spare[:, 0:2]),
                  reads=["spare"], writes=[f"msw{j}"])
            p.add("vector", lambda e, jj=j: e.tensor_copy(msw_sb[:, 3 * jj + 2:3 * jj + 3], spare[:, 2:3]),
                  reads=["spare2"], writes=[f"mswn{j}"])

        def emit_w_piece(j, i):
            nm, w_h, tr, wT, wTk = weights[j]
            pc = w_pieces(w_h, tr)[i]
            inv_ap = msw_sb[:, 3 * j:3 * j + 1]
            ninv_ap = msw_sb[:, 3 * j + 2:3 * j + 3]
            q4 = i % 4
            hsl = wbufs[q4][0][:]
            wk = wbufs[q4][1]
            p.add("sync", lambda e, pc=pc, hsl=hsl: e.dma_start(out=hsl, in_=pc),
                  writes=[wk], dma=True, ch=f"wp{q4}")
            p.add("scalar", lambda e, hsl=hsl, ninv_ap=ninv_ap: e.activation(
                s1b[:], hsl, AF.Sign, bias=nhalf_ap[:], scale=ninv_ap),
                reads=[wk, f"mswn{j}", "nhalf"], writes=["s1b"])
            p.add("scalar", lambda e, hsl=hsl, inv_ap=inv_ap: e.activation(
                ddb[:], hsl, AF.Sign, bias=nhalf_ap[:], scale=inv_ap),
                reads=[wk, f"msw{j}", "nhalf"], writes=["ddb"])
            p.add("vector", lambda e: e.tensor_tensor(
                xq4[3][:], s1b[:], ddb[:], ALU.subtract),
                reads=["ddb", "s1b"], writes=["xq4_3"])
            for blk in range(8):
                p.add("tensor", lambda e, blk=blk: e.transpose(
                    ptr[:, blk * 128:(blk + 1) * 128], xq4[3][:, blk * 128:(blk + 1) * 128], ident[:]),
                    reads=["xq4_3", "ident"], writes=["ptr"])
            if not tr:
                dst = wT[:, 0:DC, i * 128:(i + 1) * 128]
            else:
                oc, qc = divmod(i, NQ)
                dst = wT[:, qc * 8:(qc + 1) * 8, oc * 128:(oc + 1) * 128]
            p.add("vector", lambda e, dst=dst: e.tensor_copy(
                dst, ptr[:].rearrange("p (c n) -> p c n", c=8)),
                reads=["ptr"], writes=[wTk])

        w_order = [(j, i) for j in range(3) for i in range(32)]

        def emit_A_front(b, t):
            bi = t % 2
            row0 = (b * T + t) * 128
            p.add("sync", lambda e, row0=row0, bi=bi: e.dma_start(
                out=xa[bi][:], in_=x_h[row0:row0 + 128, :]),
                writes=[f"xa{bi}"], dma=True, ch=f"xa{bi}")
            p.add("scalar", lambda e, bi=bi, t=t: e.activation(
                xnf[:], xa[bi][:], AF.Identity,
                bias=nmr_all[:, t:t + 1], scale=rstd_all[:, t:t + 1]),
                reads=[f"xa{bi}", "rstd_all", "nmr_all"], writes=["xnf"])
            p.add("scalar", lambda e, bi=bi: e.activation(xn_bf[bi][:], xnf[:], AF.Identity),
                  reads=["xnf"], writes=[f"xn_bf{bi}"])
            p.add("gpsimd", lambda e, row0=row0, bi=bi: e.dma_start(
                out=scr_xn[row0:row0 + 128, :], in_=xn_bf[bi][:]),
                reads=[f"xn_bf{bi}"], writes=[f"scr_xn_{b}_{t}"], dma=True, ch=f"xnbf{bi}")
            if debug and b == 0:
                p.add("gpsimd", lambda e, t=t, bi=bi: e.dma_start(
                    out=dbg_xn[t * 128:(t + 1) * 128, :], in_=xn_bf[bi][:]),
                    reads=[f"xn_bf{bi}"], writes=[f"dxn{t}"], dma=True, ch=f"dbgxn{bi}")
            p.add("vector", lambda e: e.tensor_reduce(
                rc1[:], xnf[:], AX.X, ALU.max, apply_absolute_value=True),
                reads=["xnf"], writes=["rc1"])
            p.add("vector", lambda e, t=t: e.tensor_scalar(
                rowm_all[:, t:t + 1], rc1[:], 1e-5, None, ALU.max),
                reads=["rc1"], writes=[f"rowm{t}"])
            p.add("vector", lambda e, t=t: e.reciprocal(iv1[:], rowm_all[:, t:t + 1]),
                  reads=[f"rowm{t}"], writes=["iv1"])
            p.add("vector", lambda e, t=t: e.tensor_scalar(
                sqq_all[:, t:t + 1], iv1[:], 127.0, None, ALU.mult),
                reads=["iv1"], writes=[f"sqq{t}"])
            sq_ap = sqq_all[:, t:t + 1]
            p.add("scalar", lambda e, sq_ap=sq_ap, t=t: e.activation(
                i8b[:], xnf[:], AF.Identity, scale=sq_ap),
                reads=["xnf", f"sqq{t}"], writes=["i8b"])
            p.add("scalar", lambda e: e.copy(r0f[:], i8b[:]), reads=["i8b"], writes=["r0f"])
            p.add("vector", lambda e, sq_ap=sq_ap, t=t: e.scalar_tensor_tensor(
                ddb[:], xnf[:], sq_ap, r0f[:], ALU.mult, ALU.subtract),
                reads=["xnf", "r0f", f"sqq{t}"], writes=["ddb"])
            p.add("vector", lambda e: e.scalar_tensor_tensor(
                s1b[:], ddb[:], 0.5, r0f[:], ALU.is_gt, ALU.add),
                reads=["ddb", "r0f"], writes=["s1b"])
            p.add("vector", lambda e, bi=bi: e.scalar_tensor_tensor(
                xqn2[bi][:], ddb[:], -0.5, s1b[:], ALU.is_lt, ALU.subtract),
                reads=["ddb", "s1b"], writes=[f"xqn{bi}"])

        def emit_A_back(b, t):
            bi = t % 2
            row0 = (b * T + t) * 128
            for blk in range(8):
                p.add("tensor", lambda e, blk=blk, bi=bi: e.transpose(
                    ptr[:, blk * 128:(blk + 1) * 128], xn_bf[bi][:, blk * 128:(blk + 1) * 128], ident[:]),
                    reads=[f"xn_bf{bi}", "ident"], writes=["ptr"])
            p.add("vector", lambda e: e.tensor_copy(tT_buf[0][:], ptr[:]),
                  reads=["ptr"], writes=["tT0"])
            p.add("gpsimd", lambda e, row0=row0: e.dma_start(
                out=scr_xnT[row0:row0 + 128, :], in_=tT_buf[0][:]),
                reads=["tT0"], writes=[f"scr_xnT_{b}_{t}"], dma=True, ch="tT0")
            if debug and b == 0:
                p.add("gpsimd", lambda e, t=t: e.dma_start(
                    out=dbg_xnT[t * 128:(t + 1) * 128, :], in_=tT_buf[0][:]),
                    reads=["tT0"], writes=[f"dxnT{t}"], dma=True, ch="dbgT0")
            for blk in range(8):
                p.add("tensor", lambda e, blk=blk, bi=bi: e.transpose(
                    ptr[:, blk * 128:(blk + 1) * 128], xqn2[bi][:, blk * 128:(blk + 1) * 128], ident[:]),
                    reads=[f"xqn{bi}", "ident"], writes=["ptr"])
            p.add("vector", lambda e: e.tensor_copy(tT_buf[1][:], ptr[:]),
                  reads=["ptr"], writes=["tT1"])
            p.add("gpsimd", lambda e, row0=row0: e.dma_start(
                out=scr_xqT[row0:row0 + 128, :], in_=tT_buf[1][:]),
                reads=["tT1"], writes=[f"scr_xqT_{b}_{t}"], dma=True, ch="tT1")
            if debug and b == 0:
                p.add("gpsimd", lambda e, t=t: e.dma_start(
                    out=dbg_xqT[t * 128:(t + 1) * 128, :], in_=tT_buf[1][:]),
                    reads=["tT1"], writes=[f"dxqT{t}"], dma=True, ch="dbgT1")

        def emit_A_tile(b, t):
            emit_A_front(b, t)
            emit_A_back(b, t)

        def emit_A_stats(b):
            for t in range(T):
                bi = t % 2
                row0 = (b * T + t) * 128
                p.add("sync", lambda e, row0=row0, bi=bi: e.dma_start(
                    out=xa[bi][:], in_=x_h[row0:row0 + 128, :]),
                    writes=[f"xa{bi}"], dma=True, ch=f"xa{bi}")
                p.add("vector", lambda e, bi=bi: e.bn_stats(bn12[:, 0:6], xa[bi][:, 0:512]),
                      reads=[f"xa{bi}"], writes=["bn12a"])
                p.add("vector", lambda e, bi=bi: e.bn_stats(bn12[:, 6:12], xa[bi][:, 512:1024]),
                      reads=[f"xa{bi}"], writes=["bn12b"])
                p.add("vector", lambda e: e.bn_aggr(bnag[:], bn12[:]),
                      reads=["bn12a", "bn12b"], writes=["bnag"])
                p.add("vector", lambda e, t=t: e.tensor_copy(mean_all[:, t:t + 1], bnag[:, 0:1]),
                      reads=["bnag"], writes=["mean_all"])
                p.add("vector", lambda e, t=t: e.tensor_copy(var_all[:, t:t + 1], bnag[:, 1:2]),
                      reads=["bnag"], writes=["var_all"])
            p.add("scalar", lambda e: e.activation(sqt_all[:], var_all[:], AF.Sqrt, bias=eps_ap[:]),
                  reads=["var_all", "eps_ap"], writes=["sqt_all"])
            p.add("vector", lambda e: e.reciprocal(rstd_all[:], sqt_all[:]),
                  reads=["sqt_all"], writes=["rstd_all"])
            p.add("vector", lambda e: e.scalar_tensor_tensor(
                nmr_all[:], mean_all[:], -1.0, rstd_all[:], ALU.mult, ALU.mult),
                reads=["mean_all", "rstd_all"], writes=["nmr_all"])

        def emit_Escales(b):
            p.add("vector", lambda e: e.tensor_scalar(
                Eg_all[:], rowm_all[:], msw_sb[:, 1:2], None, ALU.mult),
                reads=[f"rowm{t}" for t in range(T)] + ["msw0"],
                writes=[f"Eg{t}" for t in range(T)])
            p.add("vector", lambda e: e.tensor_scalar(
                Eu_all[:], rowm_all[:], msw_sb[:, 4:5], None, ALU.mult),
                reads=[f"rowm{t}" for t in range(T)] + ["msw1"],
                writes=[f"Eu{t}" for t in range(T)])

        groups = [(0, 1, 2), (3, 4, 5), (6, 7)]

        kq_bufs = [(ast[0], "ast0", "ast0"), (ast[1], "ast1", "ast1"),
                   (xqT_st[0], "xqT_st0", "xq0"), (xqT_st[1], "xqT_st1", "xq1")]

        def emit_K(b):
            for gi, grp in enumerate(groups):
                for t in range(T):
                    bi = t % 2
                    row0 = (b * T + t) * 128
                    p.add("sync", lambda e, row0=row0, bi=bi: e.dma_start(
                        out=ast[bi][:], in_=scr_xn[row0:row0 + 128, :]),
                        reads=[f"scr_xn_{b}_{t}"], writes=[f"ast{bi}"], dma=True, ch=f"ast{bi}")
                    for ji, c in enumerate(grp):
                        ps = (pd, pg, pu)[ji]
                        pk = ("pd", "pg", "pu")[ji]
                        for h2 in range(2):
                            p.add("tensor", lambda e, c=c, h2=h2, bi=bi, ps=ps, t=t: e.matmul(
                                ps[:, h2 * 512:(h2 + 1) * 512],
                                ast[bi][:, c * 128:(c + 1) * 128],
                                ast[bi][:, h2 * 512:(h2 + 1) * 512],
                                start=(t == 0), stop=(t == T - 1)),
                                reads=[f"ast{bi}"], writes=[pk])
                for ji, c in enumerate(grp):
                    ps = (pd, pg, pu)[ji]
                    pk = ("pd", "pg", "pu")[ji]
                    bi = ji % 2
                    p.add("scalar", lambda e, ps=ps, bi=bi: e.copy(hqT[bi][:], ps[:]),
                          reads=[pk], writes=[f"hqT{bi}"])
                    krow = (b * DC + c) * 128
                    p.add("gpsimd", lambda e, krow=krow, bi=bi: e.dma_start(
                        out=scr_K[krow:krow + 128, :], in_=hqT[bi][:]),
                        reads=[f"hqT{bi}"], writes=[f"scr_K_{b}_{c}"], dma=True, ch=f"kev{bi}")
                    if debug and b == 0:
                        p.add("gpsimd", lambda e, c=c, bi=bi: e.dma_start(
                            out=dbg_K[c * 128:(c + 1) * 128, :], in_=hqT[bi][:]),
                            reads=[f"hqT{bi}"], writes=[f"dK{c}"], dma=True, ch=f"dbgK{bi}")

        def emit_M(b):
            for gi, grp in enumerate(groups):
                c0 = grp[0]
                w = len(grp)
                for k in range(DC):
                    bi = k % 2
                    buf, bk, bch = (ast[0], "ast0", "ast0") if k % 2 == 0 else (ast[1], "ast1", "ast1")
                    p.add("sync", lambda e, k=k, c0=c0, w=w, bi=bi: e.dma_start(
                        out=xa[bi][:, 0:w * 128], in_=q_h[k * 128:(k + 1) * 128, c0 * 128:(c0 + w) * 128]),
                        writes=[f"xa{bi}"], dma=True, ch=f"xa{bi}")
                    p.add("scalar", lambda e, bi=bi, w=w: e.activation(
                        tT_buf[bi][:, 0:w * 128], xa[bi][:, 0:w * 128], AF.Identity),
                        reads=[f"xa{bi}"], writes=[f"tT{bi}"])
                    krow = (b * DC + k) * 128
                    p.add("sync", lambda e, krow=krow, buf=buf: e.dma_start(
                        out=buf[:], in_=scr_K[krow:krow + 128, :]),
                        reads=[f"scr_K_{b}_{k}"], writes=[bk], dma=True, ch=bch)
                    for ji, c in enumerate(grp):
                        ps = (pd, pg, pu)[ji]
                        pk = ("pd", "pg", "pu")[ji]
                        for h2 in range(2):
                            p.add("tensor", lambda e, ji=ji, h2=h2, bi=bi, ps=ps, buf=buf, k=k: e.matmul(
                                ps[:, h2 * 512:(h2 + 1) * 512],
                                tT_buf[bi][:, ji * 128:(ji + 1) * 128],
                                buf[:, h2 * 512:(h2 + 1) * 512],
                                start=(k == 0), stop=(k == DC - 1)),
                                reads=[f"tT{bi}", bk], writes=[pk])
                for ji, c in enumerate(grp):
                    ps = (pd, pg, pu)[ji]
                    pk = ("pd", "pg", "pu")[ji]
                    p.add("scalar", lambda e, ps=ps, c=c: e.copy(M_sb[:, c, :], ps[:]),
                          reads=[pk], writes=[f"M{c}"])
            if debug and b == 0:
                p.add("gpsimd", lambda e: e.dma_start(
                    out=dbg_M, in_=M_sb[:].rearrange("p c d -> p (c d)")),
                    reads=[f"M{c}" for c in range(DC)], writes=["dbg3"], dma=True, ch="dbgM")

        def fs_dmas(b, t):
            bi = t % 2
            row0 = (b * T + t) * 128
            p.add("sync", lambda e, row0=row0, bi=bi: e.dma_start(
                out=xqT_st[bi][:], in_=scr_xqT[row0:row0 + 128, :]),
                reads=[f"scr_xqT_{b}_{t}"], writes=[f"xqT_st{bi}"], dma=True, ch=f"xq{bi}")
            p.add("sync", lambda e, row0=row0, bi=bi: e.dma_start(
                out=ast[bi][:], in_=scr_xnT[row0:row0 + 128, :]),
                reads=[f"scr_xnT_{b}_{t}"], writes=[f"ast{bi}"], dma=True, ch=f"ast{bi}")
            p.add("sync", lambda e, row0=row0, bi=bi: e.dma_start(
                out=x_in[bi][:], in_=x_h[row0:row0 + 128, :]),
                writes=[f"x_in{bi}"], dma=True, ch=f"xin{bi}")

        def fs_B_chain(t, qu):
            p2 = t % 2
            s2_ap = s2c[:, p2:p2 + 1]
            hq_sl = h_sb[:, qu * 1024:(qu + 1) * 1024]
            p.add("scalar", lambda e, hq_sl=hq_sl, s2_ap=s2_ap: e.activation(
                i8b[:], hq_sl, AF.Identity, scale=s2_ap),
                reads=[f"h{qu}", f"s2c{p2}"], writes=["i8b"])
            p.add("scalar", lambda e: e.copy(r0f[:], i8b[:]), reads=["i8b"], writes=["r0f"])
            p.add("vector", lambda e, hq_sl=hq_sl, s2_ap=s2_ap: e.scalar_tensor_tensor(
                ddb[:], hq_sl, s2_ap, r0f[:], ALU.mult, ALU.subtract),
                reads=[f"h{qu}", "r0f", f"s2c{p2}"], writes=["ddb"])
            p.add("vector", lambda e: e.scalar_tensor_tensor(
                s1b[:], ddb[:], 0.5, r0f[:], ALU.is_gt, ALU.add),
                reads=["ddb", "r0f"], writes=["s1b"])
            p.add("vector", lambda e, qu=qu: e.scalar_tensor_tensor(
                xq4[qu][:], ddb[:], -0.5, s1b[:], ALU.is_lt, ALU.subtract),
                reads=["ddb", "s1b"], writes=[f"xq4_{qu}"])

        def fs_gateup(b, t, qu):
            bi = t % 2
            for h2 in range(2):
                for dc in range(DC):
                    p.add("tensor", lambda e, qu=qu, h2=h2, dc=dc, bi=bi: e.matmul(
                        pg[:, h2 * 512:(h2 + 1) * 512],
                        xqT_st[bi][:, dc * 128:(dc + 1) * 128],
                        wgT[:, dc, qu * 1024 + h2 * 512: qu * 1024 + (h2 + 1) * 512],
                        start=(dc == 0), stop=(dc == DC - 1)),
                        reads=[f"xqT_st{bi}", "wgT"], writes=["pg"])
            for h2 in range(2):
                for dc in range(DC):
                    p.add("tensor", lambda e, qu=qu, h2=h2, dc=dc, bi=bi: e.matmul(
                        pu[:, h2 * 512:(h2 + 1) * 512],
                        xqT_st[bi][:, dc * 128:(dc + 1) * 128],
                        wuT[:, dc, qu * 1024 + h2 * 512: qu * 1024 + (h2 + 1) * 512],
                        start=(dc == 0), stop=(dc == DC - 1)),
                        reads=[f"xqT_st{bi}", "wuT"], writes=["pu"])

        def fs_tr(t, qu):
            for blk in range(8):
                p.add("tensor", lambda e, qu=qu, blk=blk: e.transpose(
                    ptr[:, blk * 128:(blk + 1) * 128], xq4[qu][:, blk * 128:(blk + 1) * 128], ident[:]),
                    reads=[f"xq4_{qu}", "ident"], writes=["ptr"])
            hb = qu % 2
            p.add("vector", lambda e, hb=hb: e.tensor_copy(hqT[hb][:], ptr[:]),
                  reads=["ptr"], writes=[f"hqT{hb}"])

        def fs_down(t, qu):
            hb = qu % 2
            for h2 in range(2):
                for jj in range(8):
                    p.add("tensor", lambda e, qu=qu, h2=h2, jj=jj, hb=hb: e.matmul(
                        pd[:, h2 * 512:(h2 + 1) * 512],
                        hqT[hb][:, jj * 128:(jj + 1) * 128],
                        wdT[:, qu * 8 + jj, h2 * 512:(h2 + 1) * 512],
                        start=(qu == 0 and jj == 0), stop=(qu == NQ - 1 and jj == 7)),
                        reads=[f"hqT{hb}", "wdT"], writes=["pd"])

        def fs_sgh(t, qu):
            p.add("scalar", lambda e, t=t: e.activation(
                sg_sb[:], pg[:], AF.Silu, scale=Eg_all[:, t:t + 1]),
                reads=["pg", f"Eg{t}"], writes=["sg"])
            p.add("vector", lambda e, qu=qu, t=t: e.scalar_tensor_tensor(
                h_sb[:, qu * 1024:(qu + 1) * 1024], pu[:], Eu_all[:, t:t + 1], sg_sb[:],
                ALU.mult, ALU.mult),
                reads=["pu", "sg", f"Eu{t}"], writes=[f"h{qu}"])
            p.add("vector", lambda e, qu=qu: e.tensor_reduce(
                hm4[:, qu:qu + 1], h_sb[:, qu * 1024:(qu + 1) * 1024], AX.X,
                ALU.max, apply_absolute_value=True),
                reads=[f"h{qu}"], writes=["hm4"])

        def fs_F_tail(t):
            p2 = t % 2
            p.add("vector", lambda e: e.tensor_reduce(hm1[:], hm4[:], AX.X, ALU.max),
                  reads=["hm4"], writes=["hm1"])
            p.add("vector", lambda e: e.tensor_scalar(hm1[:], hm1[:], 1e-5, None, ALU.max),
                  reads=["hm1"], writes=["hm1"])
            p.add("vector", lambda e, p2=p2: e.reciprocal(s2c[:, p2:p2 + 1], hm1[:]),
                  reads=["hm1"], writes=[f"s2c{p2}"])
            p.add("vector", lambda e, p2=p2: e.tensor_scalar(
                s2c[:, p2:p2 + 1], s2c[:, p2:p2 + 1], 127.0, None, ALU.mult),
                reads=[f"s2c{p2}"], writes=[f"s2c{p2}"])
            p.add("vector", lambda e, p2=p2: e.tensor_scalar(
                Edc[:, p2:p2 + 1], hm1[:], msw_sb[:, 7:8], None, ALU.mult),
                reads=["hm1", "msw2"], writes=[f"Edc{p2}"])

        def fs_B_g(b, t):
            p2 = t % 2
            p.add("scalar", lambda e, p2=p2: e.activation(
                g_sb[:], pd[:], AF.Silu, scale=Edc[:, p2:p2 + 1]),
                reads=["pd", f"Edc{p2}"], writes=["g_sb"])

        def fs_B_tail(b, t):
            bi = t % 2
            p2 = t % 2
            row0 = (b * T + t) * 128
            for h2 in range(2):
                for dc in range(DC):
                    p.add("tensor", lambda e, h2=h2, dc=dc, bi=bi: e.matmul(
                        spare[:],
                        ast[bi][:, dc * 128:(dc + 1) * 128],
                        M_sb[:, dc, h2 * 512:(h2 + 1) * 512],
                        start=(dc == 0), stop=(dc == DC - 1)),
                        reads=[f"ast{bi}", f"M{dc}"], writes=["spare"])
                p.add("vector", lambda e, h2=h2: e.scalar_tensor_tensor(
                    ddb[:, h2 * 512:(h2 + 1) * 512], spare[:], 1.0,
                    g_sb[:, h2 * 512:(h2 + 1) * 512], ALU.mult, ALU.mult),
                    reads=["spare", "g_sb"], writes=["ddb"])
            p.add("vector", lambda e, bi=bi: e.tensor_tensor(
                out_b[bi][:], ddb[:], x_in[bi][:], ALU.add),
                reads=["ddb", f"x_in{bi}"], writes=[f"out_b{bi}"])
            p.add("gpsimd", lambda e, row0=row0, bi=bi: e.dma_start(
                out=o_h[row0:row0 + 128, :], in_=out_b[bi][:]),
                reads=[f"out_b{bi}"], writes=[f"hbm_out_{b}_{t}"], dma=True, ch=f"out{bi}")

        def emit_FS(b, next_a):
            fs_dmas(b, 0)
            for slot in range(T + 1):
                tf, tb = slot, slot - 1
                for step in range(NQ + 1):
                    if tb >= 0 and step < NQ:
                        fs_B_chain(tb, step)
                    if step < NQ and tf < T:
                        fs_gateup(b, tf, step)
                    if tb >= 0 and step >= 1:
                        fs_down(tb, step - 1)
                    if tb >= 0 and step < NQ:
                        fs_tr(tb, step)
                    if step < NQ and tf < T:
                        fs_sgh(tf, step)
                if tf < T:
                    fs_F_tail(tf)
                if tb >= 0:
                    fs_B_g(b, tb)
                if next_a and slot >= 1:
                    emit_A_back(b + 1, slot - 1)
                if tb >= 0:
                    fs_B_tail(b, tb)
                if next_a and slot < T:
                    emit_A_front(b + 1, slot)
                if tf + 1 < T:
                    fs_dmas(b, tf + 1)

        emit_A_stats(0)
        wi = 0
        for t in range(T):
            emit_A_tile(0, t)
            for _ in range(6):
                if wi < len(w_order):
                    j1, i1 = w_order[wi]
                    emit_p1_piece(j1, i1)
                    if i1 == 31:
                        emit_p1_final(j1)
                    wi += 1
        while wi < len(w_order):
            j1, i1 = w_order[wi]
            emit_p1_piece(j1, i1)
            if i1 == 31:
                emit_p1_final(j1)
            wi += 1
        emit_Escales(0)
        for j1 in range(3):
            for i1 in range(32):
                emit_w_piece(j1, i1)
            if j1 == 0:
                emit_K(0)
            elif j1 == 1:
                emit_M(0)
        if debug:
            p.add("sync", lambda e: e.dma_start(out=dbg_msw, in_=msw_sb[:]),
                  reads=["msw0", "msw1", "msw2"], writes=["dbg2"], dma=True, ch="dbgB")
            p.add("gpsimd", lambda e: e.dma_start(
                out=dbg_wgT, in_=wgT[:].rearrange("p c d -> p (c d)")),
                reads=["wgT"], writes=["dbg1"], dma=True, ch="dbgA")
        for b in range(B):
            if b + 1 < B:
                emit_A_stats(b + 1)
            emit_FS(b, next_a=(b + 1 < B))
            if b + 1 < B:
                emit_K(b + 1)
                emit_M(b + 1)
                emit_Escales(b + 1)

        p.add("sync", lambda e: e.nop(),
              reads=[f"hbm_out_{b}_{t}" for b in range(B) for t in range(T)])

        p.emit(nc, block)
    return nc


_NC_CACHE = {}


def kernel(L0_attn_hidden_states, L0_attn_queries, L0_ffn_w_gate, L0_ffn_w_up, L0_ffn_w_down):
    x = np.ascontiguousarray(np.asarray(L0_attn_hidden_states, dtype=np.float32))
    q = np.ascontiguousarray(np.asarray(L0_attn_queries, dtype=np.float32))
    wg = np.ascontiguousarray(np.asarray(L0_ffn_w_gate, dtype=np.float32))
    wu = np.ascontiguousarray(np.asarray(L0_ffn_w_up, dtype=np.float32))
    wd = np.ascontiguousarray(np.asarray(L0_ffn_w_down, dtype=np.float32))
    xs = x.reshape(NCORES, B * 2048, D)
    if "nc" not in _NC_CACHE:
        _NC_CACHE["nc"] = build_nc()
    nc = _NC_CACHE["nc"]
    in_maps = [{"x": np.ascontiguousarray(xs[i]), "q": q, "wg": wg, "wu": wu, "wd": wd}
               for i in range(NCORES)]
    res = run_bass_kernel_spmd(nc, in_maps, core_ids=list(range(NCORES)))
    out = np.concatenate([r["o"].reshape(B, 2048, D) for r in res.results], axis=0)
    return out.astype(np.float32)


# version 26
# speedup vs baseline: 1.1911x; 1.0200x over previous
import numpy as np
import concourse.bass as bass
import concourse.mybir as mybir
from concourse.bass_utils import run_bass_kernel_spmd

F32 = mybir.dt.float32
BF16 = mybir.dt.bfloat16
FP8 = mybir.dt.float8e4
I8 = mybir.dt.int8
AF = mybir.ActivationFunctionType
ALU = mybir.AluOpType
AX = mybir.AxisListType

NCORES = 8
B = 4
T = 16
D = 1024
DC = 8
INNER = 4096
NQ = 4
ROWS = B * T * 128

ENGINES = ("sync", "scalar", "vector", "tensor", "gpsimd")


class Prog:

    def __init__(self):
        self.ops = []
        self.res = {}

    def add(self, eng, fn, reads=(), writes=(), dma=False, ch=None):
        assert (ch is not None) == bool(dma)
        idx = len(self.ops)
        deps = set()
        for k in reads:
            r = self.res.get(k)
            if r and r["w"] is not None:
                deps.add(r["w"])
        for k in writes:
            r = self.res.get(k)
            if r:
                if r["w"] is not None:
                    deps.add(r["w"])
                deps.update(r["r"])
        self.ops.append({"eng": eng, "fn": fn, "dma": dma, "ch": ch, "deps": deps})
        for k in reads:
            self.res.setdefault(k, {"w": None, "r": []})["r"].append(idx)
        for k in writes:
            self.res[k] = {"w": idx, "r": []}
        return idx

    def emit(self, nc, block):
        ops = self.ops
        n = len(ops)
        needs_sem = [set() for _ in range(n)]
        for i, op in enumerate(ops):
            for d in op["deps"]:
                p = ops[d]
                if (p["eng"] == op["eng"] == "tensor"
                        and not p["dma"] and not op["dma"]):
                    continue
                needs_sem[i].add(d)
        milestone = [False] * n
        for i in range(n):
            for d in needs_sem[i]:
                milestone[d] = True
        eng_count = {e: 0 for e in ENGINES}
        ch_count = {}
        for i, op in enumerate(ops):
            if op["dma"]:
                c = op["ch"]
                ch_count[c] = ch_count.get(c, 0) + 1
                op["sem"] = "ch_" + c
                op["val"] = 16 * ch_count[c]
                op["inc"] = 16
            elif milestone[i]:
                eng_count[op["eng"]] += 1
                op["sem"] = "eng_" + op["eng"]
                op["val"] = eng_count[op["eng"]]
                op["inc"] = 1
            else:
                op["sem"] = None
        eng_vc = {e: {} for e in ENGINES}
        waits_out = [[] for _ in range(n)]
        for i, op in enumerate(ops):
            vc = eng_vc[op["eng"]]
            req = {}
            for d in needs_sem[i]:
                pp = ops[d]
                s, v = pp["sem"], pp["val"]
                if vc.get(s, 0) >= v:
                    pass
                elif req.get(s, 0) < v:
                    req[s] = v
            for s, v in sorted(req.items()):
                waits_out[i].append((s, v))
            for d in needs_sem[i]:
                for s2, v2 in ops[d].get("vc_after", {}).items():
                    if vc.get(s2, 0) < v2:
                        vc[s2] = v2
            for s, v in req.items():
                if vc.get(s, 0) < v:
                    vc[s] = v
            va = dict(vc)
            if op["sem"] is not None:
                va[op["sem"]] = op["val"]
            op["vc_after"] = va

        sem_handles = {}
        from contextlib import ExitStack
        self._stack = ExitStack()
        for e in ENGINES:
            sem_handles["eng_" + e] = self._stack.enter_context(nc.semaphore("s_" + e))
        for c in ch_count:
            sem_handles["ch_" + c] = self._stack.enter_context(nc.semaphore("c_" + c))

        def run_engine(name):
            def f(e):
                for i, op in enumerate(ops):
                    if op["eng"] != name:
                        continue
                    for s, v in waits_out[i]:
                        e.wait_ge(sem_handles[s], v)
                    ins = op["fn"](e)
                    if op["sem"] is not None:
                        ins.then_inc(sem_handles[op["sem"]], op["inc"])
            return f

        for e in ENGINES:
            getattr(block, e)(run_engine(e))
        self._stack.close()


def build_nc(debug=False):
    nc = bass.Bass("TRN2", target_bir_lowering=False)
    x_h = nc.dram_tensor("x", [ROWS, D], F32, kind="ExternalInput").ap()
    q_h = nc.dram_tensor("q", [D, D], F32, kind="ExternalInput").ap()
    wg_h = nc.dram_tensor("wg", [INNER, D], F32, kind="ExternalInput").ap()
    wu_h = nc.dram_tensor("wu", [INNER, D], F32, kind="ExternalInput").ap()
    wd_h = nc.dram_tensor("wd", [D, INNER], F32, kind="ExternalInput").ap()
    o_h = nc.dram_tensor("o", [ROWS, D], F32, kind="ExternalOutput").ap()

    scr_xn = nc.dram_tensor("scr_xn", [B * T * 128, D], BF16, kind="Internal").ap()
    scr_xnT = nc.dram_tensor("scr_xnT", [B * T * 128, D], BF16, kind="Internal").ap()
    scr_xqT = nc.dram_tensor("scr_xqT", [B * T * 128, D], BF16, kind="Internal").ap()
    scr_K = nc.dram_tensor("scr_K", [B * DC * 128, D], BF16, kind="Internal").ap()
    if debug:
        dbg_xn = nc.dram_tensor("dbg_xn", [T * 128, D], F32, kind="ExternalOutput").ap()
        dbg_xnT = nc.dram_tensor("dbg_xnT", [T * 128, D], F32, kind="ExternalOutput").ap()
        dbg_xqT = nc.dram_tensor("dbg_xqT", [T * 128, D], F32, kind="ExternalOutput").ap()
        dbg_K = nc.dram_tensor("dbg_K", [DC * 128, D], F32, kind="ExternalOutput").ap()
        dbg_M = nc.dram_tensor("dbg_M", [128, DC * D], F32, kind="ExternalOutput").ap()
        dbg_msw = nc.dram_tensor("dbg_msw", [128, 9], F32, kind="ExternalOutput").ap()
        dbg_wgT = nc.dram_tensor("dbg_wgT", [128, DC * INNER], F32, kind="ExternalOutput").ap()

    p = Prog()
    from contextlib import ExitStack
    with ExitStack() as st:
        e_ = st.enter_context
        wgT = e_(nc.sbuf_tensor("wgT", [128, DC, INNER], FP8))
        wuT = e_(nc.sbuf_tensor("wuT", [128, DC, INNER], FP8))
        wdT = e_(nc.sbuf_tensor("wdT", [128, INNER // 128, D], FP8))
        ident = e_(nc.sbuf_tensor("ident", [128, 128], BF16))
        ones_row = e_(nc.sbuf_tensor("ones_row", [1, 128], F32))
        ones_col = e_(nc.sbuf_tensor("ones_col", [128, 1], F32))
        eps_ap = e_(nc.sbuf_tensor("eps_ap", [128, 1], F32))
        nhalf_ap = e_(nc.sbuf_tensor("nhalf_ap", [128, 1], F32))
        msw_sb = e_(nc.sbuf_tensor("msw_sb", [128, 9], F32))
        x_in = [e_(nc.sbuf_tensor(f"x_in{i}", [128, D], F32)) for i in range(2)]
        xa = [e_(nc.sbuf_tensor(f"xa{i}", [128, D], F32)) for i in range(2)]
        out_b = [e_(nc.sbuf_tensor(f"out_b{i}", [128, D], F32)) for i in range(2)]

        xnf = e_(nc.sbuf_tensor("xnf", [128, D], F32))
        xn_bf = [e_(nc.sbuf_tensor(f"xn_bf{i}", [128, D], BF16)) for i in range(2)]
        i8b = e_(nc.sbuf_tensor("i8b", [128, D], I8))
        r0f = e_(nc.sbuf_tensor("r0f", [128, D], BF16))
        ddb = e_(nc.sbuf_tensor("ddb", [128, D], BF16))
        s1b = e_(nc.sbuf_tensor("s1b", [128, D], BF16))
        xqn2 = [e_(nc.sbuf_tensor(f"xqn{i}", [128, D], BF16)) for i in range(2)]
        xq4 = [e_(nc.sbuf_tensor(f"xq4_{i}", [128, D], BF16)) for i in range(4)]
        tT_buf = [e_(nc.sbuf_tensor(f"tT_buf{i}", [128, D], BF16)) for i in range(2)]
        ast = [e_(nc.sbuf_tensor(f"ast{i}", [128, D], BF16)) for i in range(2)]
        xqT_st = [e_(nc.sbuf_tensor(f"xqT_st{i}", [128, D], BF16)) for i in range(2)]
        h_sb = e_(nc.sbuf_tensor("h_sb", [128, NQ * 1024], BF16))
        sg_sb = e_(nc.sbuf_tensor("sg_sb", [128, 1024], BF16))
        hqT = [e_(nc.sbuf_tensor(f"hqT{i}", [128, 1024], BF16)) for i in range(2)]
        g_sb = e_(nc.sbuf_tensor("g_sb", [128, 1024], BF16))
        M_sb = e_(nc.sbuf_tensor("M_sb", [128, DC, D], BF16))
        bn12 = e_(nc.sbuf_tensor("bn12", [128, 12], F32))
        bnag = e_(nc.sbuf_tensor("bnag", [128, 2], F32))
        sq1 = e_(nc.sbuf_tensor("sq1", [128, 1], F32))
        rstd_all = e_(nc.sbuf_tensor("rstd_all", [128, T], F32))
        mean_all = e_(nc.sbuf_tensor("mean_all", [128, T], F32))
        var_all = e_(nc.sbuf_tensor("var_all", [128, T], F32))
        sqt_all = e_(nc.sbuf_tensor("sqt_all", [128, T], F32))
        rowm_all = e_(nc.sbuf_tensor("rowm_all", [128, T], F32))
        nmr_all = e_(nc.sbuf_tensor("nmr_all", [128, T], F32))
        rc1 = e_(nc.sbuf_tensor("rc1", [128, 1], F32))
        iv1 = e_(nc.sbuf_tensor("iv1", [128, 1], F32))
        sqq_all = e_(nc.sbuf_tensor("sqq_all", [128, T], F32))
        Eg_all = e_(nc.sbuf_tensor("Eg_all", [128, T], F32))
        Eu_all = e_(nc.sbuf_tensor("Eu_all", [128, T], F32))
        hm4 = e_(nc.sbuf_tensor("hm4", [128, NQ], F32))
        hm1 = e_(nc.sbuf_tensor("hm1", [128, 1], F32))
        s2c = e_(nc.sbuf_tensor("s2c", [128, 2], F32))
        Edc = e_(nc.sbuf_tensor("Edc", [128, 2], F32))
        wacc = e_(nc.sbuf_tensor("wacc", [128, 2], F32))
        prd = e_(nc.sbuf_tensor("prd", [128, 1], F32))
        sc1 = e_(nc.sbuf_tensor("sc1", [1, 4], F32))
        pd = e_(nc.psum_tensor("pd", [128, 1024], F32))
        pg = e_(nc.psum_tensor("pg", [128, 1024], F32))
        pu = e_(nc.psum_tensor("pu", [128, 1024], F32))
        ptr = e_(nc.psum_tensor("ptr", [128, 1024], BF16))
        spare = e_(nc.psum_tensor("spare", [128, 512], F32))
        block = e_(nc.Block())

        wbufs = [(x_in[0], "x_in0"), (x_in[1], "x_in1"), (out_b[0], "out_b0"), (out_b[1], "out_b1")]

        p.add("vector", lambda e: e.memset(ident[:], 0.0), writes=["ident"])
        p.add("gpsimd", lambda e: e.affine_select(
            out=ident[:], in_=ident[:], compare_op=ALU.not_equal, fill=1.0,
            base=0, pattern=[[-1, 128]], channel_multiplier=1),
            reads=["ident"], writes=["ident"])
        p.add("vector", lambda e: e.memset(ones_row[:], 1.0), writes=["ones_row"])
        p.add("vector", lambda e: e.memset(ones_col[:], 1.0), writes=["ones_col"])
        p.add("vector", lambda e: e.memset(eps_ap[:], 1e-5), writes=["eps_ap"])
        p.add("vector", lambda e: e.memset(nhalf_ap[:], -0.5), writes=["nhalf"])
        _stats0_done = []

        def w_pieces(w_h, tr):
            out = []
            if not tr:
                for ic in range(INNER // 128):
                    out.append(w_h[ic * 128:(ic + 1) * 128, :])
            else:
                for oc in range(D // 128):
                    for qc in range(NQ):
                        out.append(w_h[oc * 128:(oc + 1) * 128, qc * 1024:(qc + 1) * 1024])
            return out

        weights = [("wg", wg_h, False, wgT, "wgT"), ("wu", wu_h, False, wuT, "wuT"),
                   ("wd", wd_h, True, wdT, "wdT")]

        def emit_p1_piece(j, i):
            nm, w_h, tr, wT, wTk = weights[j]
            pc = w_pieces(w_h, tr)[i]
            if i == 0:
                p.add("vector", lambda e: e.memset(wacc[:], 0.0), writes=["wacc"])
            q4 = i % 4
            hsl = wbufs[q4][0][:]
            wk = wbufs[q4][1]
            p.add("sync", lambda e, pc=pc, hsl=hsl: e.dma_start(out=hsl, in_=pc),
                  writes=[wk], dma=True, ch=f"wp{q4}")
            p.add("vector", lambda e, hsl=hsl: e.tensor_reduce(
                prd[:], hsl, AX.X, ALU.add, apply_absolute_value=True),
                reads=[wk], writes=["prd"])
            a, bcol = i % 2, (i + 1) % 2
            p.add("vector", lambda e, a=a, bcol=bcol: e.tensor_tensor(
                wacc[:, bcol:bcol + 1], wacc[:, a:a + 1], prd[:], ALU.add),
                reads=["wacc", "prd"], writes=["wacc"])

        def emit_p1_final(j):
            p.add("tensor", lambda e: e.matmul(
                spare[0:1, 0:1], wacc[:, 0:1], ones_col[:], start=True, stop=True),
                reads=["wacc", "ones_col"], writes=["spare"])
            nelem = float(INNER * D)
            p.add("vector", lambda e, nelem=nelem: e.tensor_scalar(
                sc1[0:1, 0:1], spare[0:1, 0:1], 1.0 / nelem, None, ALU.mult),
                reads=["spare"], writes=["sc1"])
            p.add("vector", lambda e: e.tensor_scalar(
                sc1[0:1, 1:2], sc1[0:1, 0:1], 1e-5, None, ALU.max),
                reads=["sc1"], writes=["sc1b"])
            p.add("vector", lambda e: e.reciprocal(sc1[0:1, 2:3], sc1[0:1, 1:2]),
                  reads=["sc1b"], writes=["sc1c"])
            p.add("vector", lambda e: e.tensor_scalar(
                sc1[0:1, 3:4], sc1[0:1, 1:2], 1.0 / 254.0, None, ALU.mult),
                reads=["sc1b"], writes=["sc1d"])
            p.add("vector", lambda e: e.tensor_scalar(
                sc1[0:1, 0:1], sc1[0:1, 2:3], -1.0, None, ALU.mult),
                reads=["sc1c"], writes=["sc1"])
            p.add("tensor", lambda e: e.matmul(
                spare[:, 0:2], ones_row[:], sc1[0:1, 2:4], start=True, stop=True),
                reads=["sc1c", "sc1d", "ones_row"], writes=["spare"])
            p.add("tensor", lambda e: e.matmul(
                spare[:, 2:3], ones_row[:], sc1[0:1, 0:1], start=True, stop=True),
                reads=["sc1", "ones_row"], writes=["spare2"])
            p.add("vector", lambda e, jj=j: e.tensor_copy(msw_sb[:, 3 * jj:3 * jj + 2], spare[:, 0:2]),
                  reads=["spare"], writes=[f"msw{j}"])
            p.add("vector", lambda e, jj=j: e.tensor_copy(msw_sb[:, 3 * jj + 2:3 * jj + 3], spare[:, 2:3]),
                  reads=["spare2"], writes=[f"mswn{j}"])

        def emit_w_piece(j, i):
            nm, w_h, tr, wT, wTk = weights[j]
            pc = w_pieces(w_h, tr)[i]
            inv_ap = msw_sb[:, 3 * j:3 * j + 1]
            ninv_ap = msw_sb[:, 3 * j + 2:3 * j + 3]
            q4 = i % 4
            hsl = wbufs[q4][0][:]
            wk = wbufs[q4][1]
            p.add("sync", lambda e, pc=pc, hsl=hsl: e.dma_start(out=hsl, in_=pc),
                  writes=[wk], dma=True, ch=f"wp{q4}")
            p.add("scalar", lambda e, hsl=hsl, ninv_ap=ninv_ap: e.activation(
                s1b[:], hsl, AF.Sign, bias=nhalf_ap[:], scale=ninv_ap),
                reads=[wk, f"mswn{j}", "nhalf"], writes=["s1b"])
            p.add("scalar", lambda e, hsl=hsl, inv_ap=inv_ap: e.activation(
                ddb[:], hsl, AF.Sign, bias=nhalf_ap[:], scale=inv_ap),
                reads=[wk, f"msw{j}", "nhalf"], writes=["ddb"])
            p.add("vector", lambda e: e.tensor_tensor(
                xq4[3][:], s1b[:], ddb[:], ALU.subtract),
                reads=["ddb", "s1b"], writes=["xq4_3"])
            for blk in range(8):
                p.add("tensor", lambda e, blk=blk: e.transpose(
                    ptr[:, blk * 128:(blk + 1) * 128], xq4[3][:, blk * 128:(blk + 1) * 128], ident[:]),
                    reads=["xq4_3", "ident"], writes=["ptr"])
            if not tr:
                dst = wT[:, 0:DC, i * 128:(i + 1) * 128]
            else:
                oc, qc = divmod(i, NQ)
                dst = wT[:, qc * 8:(qc + 1) * 8, oc * 128:(oc + 1) * 128]
            p.add("vector", lambda e, dst=dst: e.tensor_copy(
                dst, ptr[:].rearrange("p (c n) -> p c n", c=8)),
                reads=["ptr"], writes=[wTk])

        w_order = [(j, i) for j in range(3) for i in range(32)]

        def emit_A_front(b, t):
            bi = t % 2
            row0 = (b * T + t) * 128
            p.add("sync", lambda e, row0=row0, bi=bi: e.dma_start(
                out=xa[bi][:], in_=x_h[row0:row0 + 128, :]),
                writes=[f"xa{bi}"], dma=True, ch=f"xa{bi}")
            p.add("scalar", lambda e, bi=bi, t=t: e.activation(
                xnf[:], xa[bi][:], AF.Identity,
                bias=nmr_all[:, t:t + 1], scale=rstd_all[:, t:t + 1]),
                reads=[f"xa{bi}", "rstd_all", "nmr_all"], writes=["xnf"])
            p.add("scalar", lambda e, bi=bi: e.activation(xn_bf[bi][:], xnf[:], AF.Identity),
                  reads=["xnf"], writes=[f"xn_bf{bi}"])
            p.add("gpsimd", lambda e, row0=row0, bi=bi: e.dma_start(
                out=scr_xn[row0:row0 + 128, :], in_=xn_bf[bi][:]),
                reads=[f"xn_bf{bi}"], writes=[f"scr_xn_{b}_{t}"], dma=True, ch=f"xnbf{bi}")
            if debug and b == 0:
                p.add("gpsimd", lambda e, t=t, bi=bi: e.dma_start(
                    out=dbg_xn[t * 128:(t + 1) * 128, :], in_=xn_bf[bi][:]),
                    reads=[f"xn_bf{bi}"], writes=[f"dxn{t}"], dma=True, ch=f"dbgxn{bi}")
            p.add("vector", lambda e: e.tensor_reduce(
                rc1[:], xnf[:], AX.X, ALU.max, apply_absolute_value=True),
                reads=["xnf"], writes=["rc1"])
            p.add("vector", lambda e, t=t: e.tensor_scalar(
                rowm_all[:, t:t + 1], rc1[:], 1e-5, None, ALU.max),
                reads=["rc1"], writes=[f"rowm{t}"])
            p.add("vector", lambda e, t=t: e.reciprocal(iv1[:], rowm_all[:, t:t + 1]),
                  reads=[f"rowm{t}"], writes=["iv1"])
            p.add("vector", lambda e, t=t: e.tensor_scalar(
                sqq_all[:, t:t + 1], iv1[:], 127.0, None, ALU.mult),
                reads=["iv1"], writes=[f"sqq{t}"])
            sq_ap = sqq_all[:, t:t + 1]
            p.add("scalar", lambda e, sq_ap=sq_ap, t=t: e.activation(
                i8b[:], xnf[:], AF.Identity, scale=sq_ap),
                reads=["xnf", f"sqq{t}"], writes=["i8b"])
            p.add("scalar", lambda e: e.copy(r0f[:], i8b[:]), reads=["i8b"], writes=["r0f"])
            p.add("vector", lambda e, sq_ap=sq_ap, t=t: e.scalar_tensor_tensor(
                ddb[:], xnf[:], sq_ap, r0f[:], ALU.mult, ALU.subtract),
                reads=["xnf", "r0f", f"sqq{t}"], writes=["ddb"])
            p.add("vector", lambda e: e.scalar_tensor_tensor(
                s1b[:], ddb[:], 0.5, r0f[:], ALU.is_gt, ALU.add),
                reads=["ddb", "r0f"], writes=["s1b"])
            p.add("vector", lambda e, bi=bi: e.scalar_tensor_tensor(
                xqn2[bi][:], ddb[:], -0.5, s1b[:], ALU.is_lt, ALU.subtract),
                reads=["ddb", "s1b"], writes=[f"xqn{bi}"])

        def emit_A_back(b, t):
            bi = t % 2
            row0 = (b * T + t) * 128
            for blk in range(8):
                p.add("tensor", lambda e, blk=blk, bi=bi: e.transpose(
                    ptr[:, blk * 128:(blk + 1) * 128], xn_bf[bi][:, blk * 128:(blk + 1) * 128], ident[:]),
                    reads=[f"xn_bf{bi}", "ident"], writes=["ptr"])
            p.add("vector", lambda e: e.tensor_copy(tT_buf[0][:], ptr[:]),
                  reads=["ptr"], writes=["tT0"])
            p.add("gpsimd", lambda e, row0=row0: e.dma_start(
                out=scr_xnT[row0:row0 + 128, :], in_=tT_buf[0][:]),
                reads=["tT0"], writes=[f"scr_xnT_{b}_{t}"], dma=True, ch="tT0")
            if debug and b == 0:
                p.add("gpsimd", lambda e, t=t: e.dma_start(
                    out=dbg_xnT[t * 128:(t + 1) * 128, :], in_=tT_buf[0][:]),
                    reads=["tT0"], writes=[f"dxnT{t}"], dma=True, ch="dbgT0")
            for blk in range(8):
                p.add("tensor", lambda e, blk=blk, bi=bi: e.transpose(
                    ptr[:, blk * 128:(blk + 1) * 128], xqn2[bi][:, blk * 128:(blk + 1) * 128], ident[:]),
                    reads=[f"xqn{bi}", "ident"], writes=["ptr"])
            p.add("vector", lambda e: e.tensor_copy(tT_buf[1][:], ptr[:]),
                  reads=["ptr"], writes=["tT1"])
            p.add("gpsimd", lambda e, row0=row0: e.dma_start(
                out=scr_xqT[row0:row0 + 128, :], in_=tT_buf[1][:]),
                reads=["tT1"], writes=[f"scr_xqT_{b}_{t}"], dma=True, ch="tT1")
            if debug and b == 0:
                p.add("gpsimd", lambda e, t=t: e.dma_start(
                    out=dbg_xqT[t * 128:(t + 1) * 128, :], in_=tT_buf[1][:]),
                    reads=["tT1"], writes=[f"dxqT{t}"], dma=True, ch="dbgT1")

        def emit_A_tile(b, t):
            emit_A_front(b, t)
            emit_A_back(b, t)

        def emit_A_stats_tile(b, t):
            bi = (t + 1) % 2
            row0 = (b * T + t) * 128
            p.add("sync", lambda e, row0=row0, bi=bi: e.dma_start(
                out=xa[bi][:], in_=x_h[row0:row0 + 128, :]),
                writes=[f"xa{bi}"], dma=True, ch=f"xa{bi}")
            p.add("vector", lambda e, bi=bi: e.bn_stats(bn12[:, 0:6], xa[bi][:, 0:512]),
                  reads=[f"xa{bi}"], writes=["bn12a"])
            p.add("vector", lambda e, bi=bi: e.bn_stats(bn12[:, 6:12], xa[bi][:, 512:1024]),
                  reads=[f"xa{bi}"], writes=["bn12b"])
            p.add("vector", lambda e: e.bn_aggr(bnag[:], bn12[:]),
                  reads=["bn12a", "bn12b"], writes=["bnag"])
            p.add("vector", lambda e, t=t: e.tensor_copy(mean_all[:, t:t + 1], bnag[:, 0:1]),
                  reads=["bnag"], writes=["mean_all"])
            p.add("vector", lambda e, t=t: e.tensor_copy(var_all[:, t:t + 1], bnag[:, 1:2]),
                  reads=["bnag"], writes=["var_all"])

        def emit_A_stats_final(b):
            p.add("scalar", lambda e: e.activation(sqt_all[:], var_all[:], AF.Sqrt, bias=eps_ap[:]),
                  reads=["var_all", "eps_ap"], writes=["sqt_all"])
            p.add("vector", lambda e: e.reciprocal(rstd_all[:], sqt_all[:]),
                  reads=["sqt_all"], writes=["rstd_all"])
            p.add("vector", lambda e: e.scalar_tensor_tensor(
                nmr_all[:], mean_all[:], -1.0, rstd_all[:], ALU.mult, ALU.mult),
                reads=["mean_all", "rstd_all"], writes=["nmr_all"])

        def emit_A_stats(b):
            for t in range(T):
                emit_A_stats_tile(b, t)
            emit_A_stats_final(b)

        def emit_Escales(b):
            p.add("vector", lambda e: e.tensor_scalar(
                Eg_all[:], rowm_all[:], msw_sb[:, 1:2], None, ALU.mult),
                reads=[f"rowm{t}" for t in range(T)] + ["msw0"],
                writes=[f"Eg{t}" for t in range(T)])
            p.add("vector", lambda e: e.tensor_scalar(
                Eu_all[:], rowm_all[:], msw_sb[:, 4:5], None, ALU.mult),
                reads=[f"rowm{t}" for t in range(T)] + ["msw1"],
                writes=[f"Eu{t}" for t in range(T)])

        groups = [(0, 1, 2), (3, 4, 5), (6, 7)]

        kq_bufs = [(ast[0], "ast0", "ast0"), (ast[1], "ast1", "ast1"),
                   (xqT_st[0], "xqT_st0", "xq0"), (xqT_st[1], "xqT_st1", "xq1")]

        def emit_K(b):
            for gi, grp in enumerate(groups):
                for t in range(T):
                    bi = t % 2
                    row0 = (b * T + t) * 128
                    p.add("sync", lambda e, row0=row0, bi=bi: e.dma_start(
                        out=ast[bi][:], in_=scr_xn[row0:row0 + 128, :]),
                        reads=[f"scr_xn_{b}_{t}"], writes=[f"ast{bi}"], dma=True, ch=f"ast{bi}")
                    for ji, c in enumerate(grp):
                        ps = (pd, pg, pu)[ji]
                        pk = ("pd", "pg", "pu")[ji]
                        for h2 in range(2):
                            p.add("tensor", lambda e, c=c, h2=h2, bi=bi, ps=ps, t=t: e.matmul(
                                ps[:, h2 * 512:(h2 + 1) * 512],
                                ast[bi][:, c * 128:(c + 1) * 128],
                                ast[bi][:, h2 * 512:(h2 + 1) * 512],
                                start=(t == 0), stop=(t == T - 1)),
                                reads=[f"ast{bi}"], writes=[pk])
                for ji, c in enumerate(grp):
                    ps = (pd, pg, pu)[ji]
                    pk = ("pd", "pg", "pu")[ji]
                    bi = ji % 2
                    p.add("scalar", lambda e, ps=ps, bi=bi: e.copy(hqT[bi][:], ps[:]),
                          reads=[pk], writes=[f"hqT{bi}"])
                    krow = (b * DC + c) * 128
                    p.add("gpsimd", lambda e, krow=krow, bi=bi: e.dma_start(
                        out=scr_K[krow:krow + 128, :], in_=hqT[bi][:]),
                        reads=[f"hqT{bi}"], writes=[f"scr_K_{b}_{c}"], dma=True, ch=f"kev{bi}")
                    if debug and b == 0:
                        p.add("gpsimd", lambda e, c=c, bi=bi: e.dma_start(
                            out=dbg_K[c * 128:(c + 1) * 128, :], in_=hqT[bi][:]),
                            reads=[f"hqT{bi}"], writes=[f"dK{c}"], dma=True, ch=f"dbgK{bi}")

        def emit_M(b):
            for gi, grp in enumerate(groups):
                c0 = grp[0]
                w = len(grp)
                for k in range(DC):
                    bi = k % 2
                    buf, bk, bch = (ast[0], "ast0", "ast0") if k % 2 == 0 else (ast[1], "ast1", "ast1")
                    p.add("sync", lambda e, k=k, c0=c0, w=w, bi=bi: e.dma_start(
                        out=xa[bi][:, 0:w * 128], in_=q_h[k * 128:(k + 1) * 128, c0 * 128:(c0 + w) * 128]),
                        writes=[f"xa{bi}"], dma=True, ch=f"xa{bi}")
                    p.add("scalar", lambda e, bi=bi, w=w: e.activation(
                        tT_buf[bi][:, 0:w * 128], xa[bi][:, 0:w * 128], AF.Identity),
                        reads=[f"xa{bi}"], writes=[f"tT{bi}"])
                    krow = (b * DC + k) * 128
                    p.add("sync", lambda e, krow=krow, buf=buf: e.dma_start(
                        out=buf[:], in_=scr_K[krow:krow + 128, :]),
                        reads=[f"scr_K_{b}_{k}"], writes=[bk], dma=True, ch=bch)
                    for ji, c in enumerate(grp):
                        ps = (pd, pg, pu)[ji]
                        pk = ("pd", "pg", "pu")[ji]
                        for h2 in range(2):
                            p.add("tensor", lambda e, ji=ji, h2=h2, bi=bi, ps=ps, buf=buf, k=k: e.matmul(
                                ps[:, h2 * 512:(h2 + 1) * 512],
                                tT_buf[bi][:, ji * 128:(ji + 1) * 128],
                                buf[:, h2 * 512:(h2 + 1) * 512],
                                start=(k == 0), stop=(k == DC - 1)),
                                reads=[f"tT{bi}", bk], writes=[pk])
                for ji, c in enumerate(grp):
                    ps = (pd, pg, pu)[ji]
                    pk = ("pd", "pg", "pu")[ji]
                    p.add("scalar", lambda e, ps=ps, c=c: e.copy(M_sb[:, c, :], ps[:]),
                          reads=[pk], writes=[f"M{c}"])
            if debug and b == 0:
                p.add("gpsimd", lambda e: e.dma_start(
                    out=dbg_M, in_=M_sb[:].rearrange("p c d -> p (c d)")),
                    reads=[f"M{c}" for c in range(DC)], writes=["dbg3"], dma=True, ch="dbgM")

        def fs_dmas(b, t):
            bi = t % 2
            row0 = (b * T + t) * 128
            p.add("sync", lambda e, row0=row0, bi=bi: e.dma_start(
                out=xqT_st[bi][:], in_=scr_xqT[row0:row0 + 128, :]),
                reads=[f"scr_xqT_{b}_{t}"], writes=[f"xqT_st{bi}"], dma=True, ch=f"xq{bi}")
            p.add("sync", lambda e, row0=row0, bi=bi: e.dma_start(
                out=ast[bi][:], in_=scr_xnT[row0:row0 + 128, :]),
                reads=[f"scr_xnT_{b}_{t}"], writes=[f"ast{bi}"], dma=True, ch=f"ast{bi}")
            p.add("sync", lambda e, row0=row0, bi=bi: e.dma_start(
                out=x_in[bi][:], in_=x_h[row0:row0 + 128, :]),
                writes=[f"x_in{bi}"], dma=True, ch=f"xin{bi}")

        def fs_B_chain(t, qu):
            p2 = t % 2
            s2_ap = s2c[:, p2:p2 + 1]
            hq_sl = h_sb[:, qu * 1024:(qu + 1) * 1024]
            p.add("scalar", lambda e, hq_sl=hq_sl, s2_ap=s2_ap: e.activation(
                i8b[:], hq_sl, AF.Identity, scale=s2_ap),
                reads=[f"h{qu}", f"s2c{p2}"], writes=["i8b"])
            p.add("scalar", lambda e: e.copy(r0f[:], i8b[:]), reads=["i8b"], writes=["r0f"])
            p.add("vector", lambda e, hq_sl=hq_sl, s2_ap=s2_ap: e.scalar_tensor_tensor(
                ddb[:], hq_sl, s2_ap, r0f[:], ALU.mult, ALU.subtract),
                reads=[f"h{qu}", "r0f", f"s2c{p2}"], writes=["ddb"])
            p.add("vector", lambda e: e.scalar_tensor_tensor(
                s1b[:], ddb[:], 0.5, r0f[:], ALU.is_gt, ALU.add),
                reads=["ddb", "r0f"], writes=["s1b"])
            p.add("vector", lambda e, qu=qu: e.scalar_tensor_tensor(
                xq4[qu][:], ddb[:], -0.5, s1b[:], ALU.is_lt, ALU.subtract),
                reads=["ddb", "s1b"], writes=[f"xq4_{qu}"])

        def fs_gateup(b, t, qu):
            bi = t % 2
            for h2 in range(2):
                for dc in range(DC):
                    p.add("tensor", lambda e, qu=qu, h2=h2, dc=dc, bi=bi: e.matmul(
                        pg[:, h2 * 512:(h2 + 1) * 512],
                        xqT_st[bi][:, dc * 128:(dc + 1) * 128],
                        wgT[:, dc, qu * 1024 + h2 * 512: qu * 1024 + (h2 + 1) * 512],
                        start=(dc == 0), stop=(dc == DC - 1)),
                        reads=[f"xqT_st{bi}", "wgT"], writes=["pg"])
            for h2 in range(2):
                for dc in range(DC):
                    p.add("tensor", lambda e, qu=qu, h2=h2, dc=dc, bi=bi: e.matmul(
                        pu[:, h2 * 512:(h2 + 1) * 512],
                        xqT_st[bi][:, dc * 128:(dc + 1) * 128],
                        wuT[:, dc, qu * 1024 + h2 * 512: qu * 1024 + (h2 + 1) * 512],
                        start=(dc == 0), stop=(dc == DC - 1)),
                        reads=[f"xqT_st{bi}", "wuT"], writes=["pu"])

        def fs_tr(t, qu):
            for blk in range(8):
                p.add("tensor", lambda e, qu=qu, blk=blk: e.transpose(
                    ptr[:, blk * 128:(blk + 1) * 128], xq4[qu][:, blk * 128:(blk + 1) * 128], ident[:]),
                    reads=[f"xq4_{qu}", "ident"], writes=["ptr"])
            hb = qu % 2
            p.add("vector", lambda e, hb=hb: e.tensor_copy(hqT[hb][:], ptr[:]),
                  reads=["ptr"], writes=[f"hqT{hb}"])

        def fs_down(t, qu):
            hb = qu % 2
            for h2 in range(2):
                for jj in range(8):
                    p.add("tensor", lambda e, qu=qu, h2=h2, jj=jj, hb=hb: e.matmul(
                        pd[:, h2 * 512:(h2 + 1) * 512],
                        hqT[hb][:, jj * 128:(jj + 1) * 128],
                        wdT[:, qu * 8 + jj, h2 * 512:(h2 + 1) * 512],
                        start=(qu == 0 and jj == 0), stop=(qu == NQ - 1 and jj == 7)),
                        reads=[f"hqT{hb}", "wdT"], writes=["pd"])

        def fs_sgh(t, qu):
            p.add("scalar", lambda e, t=t: e.activation(
                sg_sb[:], pg[:], AF.Silu, scale=Eg_all[:, t:t + 1]),
                reads=["pg", f"Eg{t}"], writes=["sg"])
            p.add("vector", lambda e, qu=qu, t=t: e.scalar_tensor_tensor(
                h_sb[:, qu * 1024:(qu + 1) * 1024], pu[:], Eu_all[:, t:t + 1], sg_sb[:],
                ALU.mult, ALU.mult),
                reads=["pu", "sg", f"Eu{t}"], writes=[f"h{qu}"])
            p.add("vector", lambda e, qu=qu: e.tensor_reduce(
                hm4[:, qu:qu + 1], h_sb[:, qu * 1024:(qu + 1) * 1024], AX.X,
                ALU.max, apply_absolute_value=True),
                reads=[f"h{qu}"], writes=["hm4"])

        def fs_F_tail(t):
            p2 = t % 2
            p.add("vector", lambda e: e.tensor_reduce(hm1[:], hm4[:], AX.X, ALU.max),
                  reads=["hm4"], writes=["hm1"])
            p.add("vector", lambda e: e.tensor_scalar(hm1[:], hm1[:], 1e-5, None, ALU.max),
                  reads=["hm1"], writes=["hm1"])
            p.add("vector", lambda e, p2=p2: e.reciprocal(s2c[:, p2:p2 + 1], hm1[:]),
                  reads=["hm1"], writes=[f"s2c{p2}"])
            p.add("vector", lambda e, p2=p2: e.tensor_scalar(
                s2c[:, p2:p2 + 1], s2c[:, p2:p2 + 1], 127.0, None, ALU.mult),
                reads=[f"s2c{p2}"], writes=[f"s2c{p2}"])
            p.add("vector", lambda e, p2=p2: e.tensor_scalar(
                Edc[:, p2:p2 + 1], hm1[:], msw_sb[:, 7:8], None, ALU.mult),
                reads=["hm1", "msw2"], writes=[f"Edc{p2}"])

        def fs_B_g(b, t):
            p2 = t % 2
            p.add("scalar", lambda e, p2=p2: e.activation(
                g_sb[:], pd[:], AF.Silu, scale=Edc[:, p2:p2 + 1]),
                reads=["pd", f"Edc{p2}"], writes=["g_sb"])

        def fs_B_tail(b, t):
            bi = t % 2
            p2 = t % 2
            row0 = (b * T + t) * 128
            for h2 in range(2):
                for dc in range(DC):
                    p.add("tensor", lambda e, h2=h2, dc=dc, bi=bi: e.matmul(
                        spare[:],
                        ast[bi][:, dc * 128:(dc + 1) * 128],
                        M_sb[:, dc, h2 * 512:(h2 + 1) * 512],
                        start=(dc == 0), stop=(dc == DC - 1)),
                        reads=[f"ast{bi}", f"M{dc}"], writes=["spare"])
                p.add("vector", lambda e, h2=h2: e.scalar_tensor_tensor(
                    ddb[:, h2 * 512:(h2 + 1) * 512], spare[:], 1.0,
                    g_sb[:, h2 * 512:(h2 + 1) * 512], ALU.mult, ALU.mult),
                    reads=["spare", "g_sb"], writes=["ddb"])
            p.add("vector", lambda e, bi=bi: e.tensor_tensor(
                out_b[bi][:], ddb[:], x_in[bi][:], ALU.add),
                reads=["ddb", f"x_in{bi}"], writes=[f"out_b{bi}"])
            p.add("gpsimd", lambda e, row0=row0, bi=bi: e.dma_start(
                out=o_h[row0:row0 + 128, :], in_=out_b[bi][:]),
                reads=[f"out_b{bi}"], writes=[f"hbm_out_{b}_{t}"], dma=True, ch=f"out{bi}")

        def emit_FS(b, next_a, stats_b=None):
            fs_dmas(b, 0)
            for slot in range(T + 1):
                tf, tb = slot, slot - 1
                for step in range(NQ + 1):
                    if tb >= 0 and step < NQ:
                        fs_B_chain(tb, step)
                    if step < NQ and tf < T:
                        fs_gateup(b, tf, step)
                    if tb >= 0 and step >= 1:
                        fs_down(tb, step - 1)
                    if tb >= 0 and step < NQ:
                        fs_tr(tb, step)
                    if step < NQ and tf < T:
                        fs_sgh(tf, step)
                if tf < T:
                    fs_F_tail(tf)
                if tb >= 0:
                    fs_B_g(b, tb)
                if next_a and slot >= 1:
                    emit_A_back(b + 1, slot - 1)
                if tb >= 0:
                    fs_B_tail(b, tb)
                if next_a and slot < T:
                    emit_A_front(b + 1, slot)
                if stats_b is not None and slot < T:
                    emit_A_stats_tile(stats_b, slot)
                if tf + 1 < T:
                    fs_dmas(b, tf + 1)
            if stats_b is not None:
                emit_A_stats_final(stats_b)

        emit_A_stats(0)
        wi = 0
        for t in range(T):
            emit_A_tile(0, t)
            for _ in range(6):
                if wi < len(w_order):
                    j1, i1 = w_order[wi]
                    emit_p1_piece(j1, i1)
                    if i1 == 31:
                        emit_p1_final(j1)
                    wi += 1
        while wi < len(w_order):
            j1, i1 = w_order[wi]
            emit_p1_piece(j1, i1)
            if i1 == 31:
                emit_p1_final(j1)
            wi += 1
        emit_Escales(0)
        st1 = 0
        for j1 in range(3):
            for i1 in range(32):
                emit_w_piece(j1, i1)
                if j1 >= 1 and i1 % 4 == 0 and st1 < T:
                    emit_A_stats_tile(1, st1)
                    st1 += 1
            if j1 == 0:
                emit_K(0)
            elif j1 == 1:
                emit_M(0)
        while st1 < T:
            emit_A_stats_tile(1, st1)
            st1 += 1
        emit_A_stats_final(1)
        if debug:
            p.add("sync", lambda e: e.dma_start(out=dbg_msw, in_=msw_sb[:]),
                  reads=["msw0", "msw1", "msw2"], writes=["dbg2"], dma=True, ch="dbgB")
            p.add("gpsimd", lambda e: e.dma_start(
                out=dbg_wgT, in_=wgT[:].rearrange("p c d -> p (c d)")),
                reads=["wgT"], writes=["dbg1"], dma=True, ch="dbgA")
        for b in range(B):
            emit_FS(b, next_a=(b + 1 < B),
                    stats_b=(b + 2 if b + 2 < B else None))
            if b + 1 < B:
                emit_K(b + 1)
                emit_M(b + 1)
                emit_Escales(b + 1)

        p.add("sync", lambda e: e.nop(),
              reads=[f"hbm_out_{b}_{t}" for b in range(B) for t in range(T)])

        p.emit(nc, block)
    return nc


_NC_CACHE = {}


def kernel(L0_attn_hidden_states, L0_attn_queries, L0_ffn_w_gate, L0_ffn_w_up, L0_ffn_w_down):
    x = np.ascontiguousarray(np.asarray(L0_attn_hidden_states, dtype=np.float32))
    q = np.ascontiguousarray(np.asarray(L0_attn_queries, dtype=np.float32))
    wg = np.ascontiguousarray(np.asarray(L0_ffn_w_gate, dtype=np.float32))
    wu = np.ascontiguousarray(np.asarray(L0_ffn_w_up, dtype=np.float32))
    wd = np.ascontiguousarray(np.asarray(L0_ffn_w_down, dtype=np.float32))
    xs = x.reshape(NCORES, B * 2048, D)
    if "nc" not in _NC_CACHE:
        _NC_CACHE["nc"] = build_nc()
    nc = _NC_CACHE["nc"]
    in_maps = [{"x": np.ascontiguousarray(xs[i]), "q": q, "wg": wg, "wu": wu, "wd": wd}
               for i in range(NCORES)]
    res = run_bass_kernel_spmd(nc, in_maps, core_ids=list(range(NCORES)))
    out = np.concatenate([r["o"].reshape(B, 2048, D) for r in res.results], axis=0)
    return out.astype(np.float32)
